# version 06
# baseline (speedup 1.0000x reference)
import contextlib
import numpy as np
import ml_dtypes
import concourse.bass as bass
import concourse.mybir as mybir
from concourse.bass_utils import run_bass_kernel_spmd

F32 = mybir.dt.float32
BF16 = mybir.dt.bfloat16
I8 = mybir.dt.int8
U8 = mybir.dt.uint8
AF = mybir.ActivationFunctionType
ALU = mybir.AluOpType
AX = mybir.AxisListType.X

C = 2048
TOWN = 2048
TPAD = 2176
NCC = 16
NTC = 16
NTCP = 17
L = 128
NCH = 16
H, HN = 32, 64
DI = 8192
NOC = 64
EPS_LN = 1e-5
EPS_GN = 1e-5 * 64.0

ENG = ("sync", "gpsimd", "tensor", "vector", "scalar")
USE_CC = True
P4_NCH = 99
P6_PARTS = frozenset(['corr','mm','yf','gn','yg','tr','wo'])
SINIT_ZERO = False
P4_PARTS = frozenset(['dec','elem','tr','D','s0','heads','supd'])


class Plan:

    def __init__(self, nc):
        self.nc = nc
        self.cnt = {}
        self.q = {e: [] for e in ENG}

    def val(self, s):
        return self.cnt.get(s, 0)

    def plan(self, eng, build=None, waits=(), inc=None):
        waits = tuple((s, v) for (s, v) in waits if v > 0)
        for s, v in waits:
            assert v <= self.val(s), f"wait on future value {v} > {self.val(s)}"
        if inc is not None:
            s, n = inc
            self.cnt[s] = self.val(s) + n
        self.q[eng].append((waits, build, inc))
        return self.val(inc[0]) if inc is not None else None

    def vdrain(self):
        self.plan("vector", lambda v: v.drain())

    def sdrain(self):
        self.plan("scalar", lambda sc: sc.drain())

    def dma(self, eng, out, in_, sem, waits=()):
        return self.plan(eng, (lambda e, o=out, i=in_: e.dma_start(out=o, in_=i)),
                         waits=waits, inc=(sem, 16))

    def barrier(self, sem):
        for e in ENG:
            self.plan(e, None, inc=(sem, 1))
        tot = self.val(sem)
        for e in ENG:
            self.plan(e, None, waits=[(sem, tot)])

    def flush(self, block):
        for e in ENG:
            ops = self.q[e]
            if not ops:
                continue
            self.q[e] = []

            def runner(engine, ops=ops):
                for waits, build, inc in ops:
                    for s, v in waits:
                        engine.wait_ge(s, v)
                    if build is not None:
                        ins = build(engine)
                        if inc is not None:
                            ins.then_inc(*inc)
                    elif inc is not None:
                        engine.sem_inc(inc[0], inc[1])

            getattr(block, e)(runner)


def r3(t, w):
    return t[:, :].rearrange("p (a t) -> p a t", t=w)


class Builder:
    def __init__(self, taps=(), stop_after=99):
        self.stop_after = stop_after
        nc = bass.Bass("TRN2", target_bir_lowering=False)
        self.nc = nc
        self.p = Plan(nc)
        taps = set(taps)
        self._taps = taps

        din = lambda n, sh, dt: nc.dram_tensor(
            n, list(sh), dt, kind="ExternalInput").ap()

        def scr(n, sh, dt):
            kind = "ExternalOutput" if n in taps else "Internal"
            return nc.dram_tensor(n, list(sh), dt, kind=kind).ap()

        self.x_ext = din("x_ext", (TPAD, C), F32)
        self.wq = {q: din(f"w_{q}", (C, C), F32) for q in "rkvg"}
        self.w_o = din("w_o", (C, C), F32)
        self.w1 = din("w1", (C, 160), F32)
        self.w2r = din("w2r", (32, 5 * C), F32)
        self.td1 = din("td1", (C, 64), F32)
        self.td2 = din("td2", (64, C), F32)
        self.maa = din("maa", (128, 16 * 7), F32)
        self.u_in = din("u_in", (128, 16), F32)
        self.lnw_in = din("lnw_bc", (128, C), F32)
        self.lnb_in = din("lnb_bc", (128, C), F32)
        self.sdown_in = din("sdown_bc", (128, C), F32)
        self.sg_in = din("sg_in", (128, 64), F32)
        self.su_in = din("su_in", (128, 64), F32)
        self.wgate = din("wgate", (16, C, 512), I8)
        self.wup = din("wup", (16, C, 512), I8)
        self.wdown = din("wdown", (DI, C), I8)
        self.cmask = din("cmask", (128, 1), F32)
        self.imask = din("imask", (128, 1), F32)

        self.out = nc.dram_tensor("out", [TOWN, C], F32, kind="ExternalOutput").ap()

        self.sp_xq = {q: scr(f"sp_xq_{q}", (NCC, 128, TOWN), BF16) for q in "wkvrg"}
        self.sp_xnb = scr("sp_xnb", (NCC, 128, TOWN), BF16)
        self.sp_e = scr("sp_e", (NCC, 128, TOWN), F32)
        self.sp_r = scr("sp_r", (NCC, 128, TOWN), BF16)
        self.sp_k = scr("sp_k", (NCC, 128, TOWN), BF16)
        self.sp_v = scr("sp_v", (NTC, 128, C), BF16)
        self.sp_g = scr("sp_g", (NTC, 128, C), BF16)
        self.sp_y = scr("sp_y", (NTC, 128, C), F32)
        self.sp_rp = scr("sp_rp", (NCH, 128, C), BF16)
        self.sp_part = scr("sp_part", (NTC, 128, C), F32)
        self.sp_fuse = scr("sp_fuse", (NOC, 128, TOWN), BF16)
        self.sp_sown = scr("sp_sown", (128, 1024), F32)
        self.sp_stat = scr("sp_stat", (NTCP, 128, 32), F32)
        self.sp_sall = scr("sp_sall", (NCH, 128, 1024), F32)
        self.sp_dec = scr("sp_dec", (NCH, 128, 64), F32)
        self.sp_ktm = scr("sp_ktm", (NCH, 128, C), BF16)
        self.sp_praw = scr("sp_praw", (NCH, 128, 1024), F32)
        self.sp_alx = scr("sp_alx", (NCH, 128, 1024), F32)
        self.sp_st2 = scr("sp_st2", (NCH, 128, 1024), F32)
        self.sp_stm = scr("sp_stm", (NCH, 128, 1024), F32)
        self.sp_ktf = scr("sp_ktf", (NCH, 128, C), BF16)
        self.cc_in = scr("cc_in", (128, 1024), F32)
        self.cc_out = scr("cc_out", (128, 1024), F32)

        self.c_idbf = nc.inline_tensor(
            np.eye(128, dtype=ml_dtypes.bfloat16), "c_idbf").ap()
        self.c_idf32 = nc.inline_tensor(
            np.eye(128, dtype=np.float32), "c_idf32").ap()
        self.c_masksu = nc.inline_tensor(
            np.triu(np.ones((128, 128), dtype=np.uint8), 1), "c_masksu").ap()
        mrow = np.zeros((128, 2), dtype=np.float32)
        mrow[0:64, 0] = 1.0
        mrow[64:128, 1] = 1.0
        self.c_mrow = nc.inline_tensor(mrow, "c_mrow").ap()

    def sb(self, name, shape, dt, stack):
        return stack.enter_context(self.nc.sbuf_tensor(name, list(shape), dt))

    def ps(self, name, shape, dt, stack):
        return stack.enter_context(self.nc.psum_tensor(name, list(shape), dt))

    def build(self):
        nc, p = self.nc, self.p
        gstack = contextlib.ExitStack()
        with (
            nc.semaphore("sD") as sD,
            nc.semaphore("sG") as sG,
            nc.semaphore("sT") as sT,
            nc.semaphore("sV") as sV,
            nc.semaphore("sS") as sS,
            nc.semaphore("sC") as sC,
            nc.semaphore("sB") as sB,
            nc.Block() as block,
        ):
            self.sD, self.sG, self.sT, self.sV, self.sS = sD, sG, sT, sV, sS
            self.sC, self.sB = sC, sB

            g = gstack
            self.idbf = self.sb("idbf", (128, 128), BF16, g)
            self.idf32 = self.sb("idf32", (128, 128), F32, g)
            self.masksu = self.sb("masksu", (128, 128), U8, g)
            self.maa_sb = self.sb("maa_sb", (128, 16 * 7), F32, g)
            self.u_sb = self.sb("u_sb", (128, 16), F32, g)
            self.cmask_sb = self.sb("cmask_sb", (128, 1), F32, g)
            self.imask_sb = self.sb("imask_sb", (128, 1), F32, g)
            self.ones = self.sb("ones", (128, 128), F32, g)
            self.mrow = self.sb("mrow", (128, 2), F32, g)

            self.xnT_stack = contextlib.ExitStack()
            self.xnT = self.sb("xnT", (128, NCC * TPAD), F32, self.xnT_stack)

            self.p0_consts()
            if self.stop_after >= 1:
                self.p1_ln()
                p.barrier(sB)
            if self.stop_after >= 2:
                self.p2_mix()
                p.barrier(sB)
            self.xnT_stack.close()
            if self.stop_after >= 3:
                self.p3_proj()
                p.barrier(sB)
            if self.stop_after >= 4:
                self.p4_scan()
                p.barrier(sB)
            if self.stop_after >= 5:
                self.p5_ffn1()
                p.barrier(sB)
            if self.stop_after >= 6:
                self.p6_gn()
                p.barrier(sB)
            if self.stop_after >= 7:
                self.p7_ffn2()
            p.flush(block)
            gstack.close()
        return nc

    def maa_col(self, j, cc):
        return self.maa_sb[:, cc * 7 + j: cc * 7 + j + 1]

    def p0_consts(self):
        p, sD, sV = self.p, self.sD, self.sV
        p.dma("sync", self.idbf[:, :], self.c_idbf, sD)
        p.dma("sync", self.idf32[:, :], self.c_idf32, sD)
        p.dma("sync", self.masksu[:, :], self.c_masksu, sD)
        p.dma("sync", self.maa_sb[:, :], self.maa, sD)
        p.dma("sync", self.u_sb[:, :], self.u_in, sD)
        p.dma("sync", self.cmask_sb[:, :], self.cmask, sD)
        p.dma("sync", self.mrow[:, :], self.c_mrow, sD)
        p.dma("sync", self.imask_sb[:, :], self.imask, sD)
        d = p.val(sD)
        p.plan("vector", lambda v: v.memset(self.ones[:, :], 1.0), inc=(sV, 1))
        vv = p.val(sV)
        for e in ENG:
            p.plan(e, None, waits=[(sD, d), (sV, vv)])

    def p1_ln(self):
        nc, p = self.nc, self.p
        sD, sT, sV, sS = self.sD, self.sT, self.sV, self.sS
        es = contextlib.ExitStack()
        xt = [self.sb(f"xt{i}", (128, C), F32, es) for i in range(2)]
        stat = self.sb("stat", (128, 32), F32, es)
        dump = self.sb("dump", (128, C), F32, es)
        pst = [self.ps(f"p1t{i}", (128, 512), F32, es) for i in range(4)]
        xnT3 = r3(self.xnT, TPAD)

        tr_tok = [0, 0]
        evac_tok = [None, None, None, None]
        for tc in range(NTCP):
            bi = tc % 2
            buf = xt[bi]
            so = bi * 16
            dv = p.dma("sync", buf[:, :], self.x_ext[tc * 128:(tc + 1) * 128, :],
                       sD, waits=[(sT, tr_tok[bi])])
            p.plan("vector", lambda v, b=buf, so=so: v.reduce_sum(
                stat[:, so:so + 1], b[:, :], axis=AX), waits=[(sD, dv)])
            p.plan("vector", lambda v, b=buf: v.tensor_mul(
                dump[:, :], b[:, :], b[:, :]))
            p.plan("vector", lambda v, so=so: v.reduce_sum(
                stat[:, so + 1:so + 2], dump[:, :], axis=AX))
            p.vdrain()
            p.plan("vector", lambda v, so=so: v.tensor_scalar(
                stat[:, so + 2:so + 3], stat[:, so:so + 1], 1.0 / C, None,
                ALU.mult))
            p.plan("vector", lambda v, so=so: v.tensor_scalar(
                stat[:, so + 3:so + 4], stat[:, so + 1:so + 2], 1.0 / C, None,
                ALU.mult))
            p.vdrain()
            p.plan("vector", lambda v, so=so: v.tensor_mul(
                stat[:, so + 4:so + 5], stat[:, so + 2:so + 3],
                stat[:, so + 2:so + 3]))
            p.vdrain()
            p.plan("vector", lambda v, so=so: v.tensor_sub(
                stat[:, so + 5:so + 6], stat[:, so + 3:so + 4],
                stat[:, so + 4:so + 5]))
            p.vdrain()
            p.plan("vector", lambda v, so=so: v.tensor_scalar(
                stat[:, so + 6:so + 7], stat[:, so + 5:so + 6], EPS_LN, None,
                ALU.add), inc=(sV, 1))
            vv = p.val(sV)
            p.plan("scalar", lambda s, so=so: s.activation(
                stat[:, so + 7:so + 8], stat[:, so + 6:so + 7], AF.Sqrt),
                waits=[(sV, vv)], inc=(sS, 1))
            sv = p.val(sS)
            p.plan("vector", lambda v, so=so: v.reciprocal(
                stat[:, so + 8:so + 9], stat[:, so + 7:so + 8]),
                waits=[(sS, sv)])
            p.vdrain()
            p.plan("vector", lambda v, so=so: v.tensor_mul(
                stat[:, so + 9:so + 10], stat[:, so + 2:so + 3],
                stat[:, so + 8:so + 9]))
            p.vdrain()
            p.plan("vector", lambda v, so=so: v.tensor_scalar(
                stat[:, so + 10:so + 11], stat[:, so + 9:so + 10], -1.0, None,
                ALU.mult))
            p.vdrain()
            p.plan("vector", lambda v, b=buf, so=so: v.tensor_scalar(
                b[:, :], b[:, :], stat[:, so + 8:so + 9],
                stat[:, so + 10:so + 11], ALU.mult, ALU.add), inc=(sV, 1))
            ntok = p.val(sV)
            for bk in range(4):
                for q4 in range(4):
                    cc = bk * 4 + q4
                    w = [(sV, ntok)]
                    if q4 == 0 and evac_tok[bk] is not None:
                        w.append(evac_tok[bk])
                    p.plan("tensor", lambda t, b=buf, cc=cc, bk=bk, q4=q4:
                           t.transpose(pst[bk][:, q4 * 128:(q4 + 1) * 128],
                                       b[:, cc * 128:(cc + 1) * 128],
                                       self.idf32[:, :]),
                           waits=w, inc=((sT, 1) if q4 == 3 else None))
                tb = p.val(sT)
                if bk % 2 == 0:
                    p.plan("vector", lambda v, bk=bk, tc=tc: v.tensor_copy(
                        xnT3[:, bk * 4:(bk + 1) * 4, tc * 128:(tc + 1) * 128],
                        pst[bk][:, :]), waits=[(sT, tb)], inc=(sV, 1))
                    evac_tok[bk] = (sV, p.val(sV))
                else:
                    p.plan("scalar", lambda s, bk=bk, tc=tc: s.activation(
                        xnT3[:, bk * 4:(bk + 1) * 4, tc * 128:(tc + 1) * 128],
                        pst[bk][:, :], AF.Copy), waits=[(sT, tb)], inc=(sS, 1))
                    evac_tok[bk] = (sS, p.val(sS))
            tr_tok[bi] = p.val(sT)
            if "sp_stat" in getattr(self, "_taps", ()):
                p.dma("sync", self.sp_stat[tc, :, :], stat[:, :], sD,
                      waits=[(sV, ntok)])
        es.close()

    def p2_mix(self):
        nc, p = self.nc, self.p
        sD, sG, sT, sV, sS = self.sD, self.sG, self.sT, self.sV, self.sS
        es = contextlib.ExitStack()
        w1_sb = self.sb("w1_sb", (128, 16 * 160), BF16, es)
        w2_sb = self.sb("w2_sb", (32, 5 * C), BF16, es)
        m_sb = self.sb("m_sb", (32, 5 * C), BF16, es)
        xxs = self.sb("xxs", (128, C), F32, es)
        xxxb = self.sb("xxxb", (128, 512), BF16, es)
        xqa = [self.sb(f"xqa{i}", (128, C), BF16, es) for i in range(2)]
        mixc = [self.sb(f"mixc{i}", (128, 512), BF16, es) for i in range(2)]
        t1b = self.sb("t1b", (128, 512), F32, es)
        ps_m = [self.ps(f"p2q{i}", (32, 512), F32, es) for i in range(5)]
        ps_mix = [self.ps(f"p2m{i}", (128, 512), F32, es) for i in range(2)]
        xnT3 = r3(self.xnT, TPAD)
        w1_3 = r3(w1_sb, 160)
        xxs3 = r3(xxs, 512)

        p.dma("gpsimd", w1_3[:, :, :],
              self.w1.rearrange("(a p) m -> p a m", p=128), sG)
        gd = p.dma("gpsimd", w2_sb[:, :], self.w2r, sG)

        m_ev = 0
        xxx_tok = 0
        for nch in range(4):
            t0 = 128 + nch * 512
            for cc in range(NCC):
                p.plan("vector", lambda v, cc=cc, t0=t0: v.tensor_sub(
                    xxs3[:, 0, :], xnT3[:, cc, t0 - 1:t0 + 511],
                    xnT3[:, cc, t0:t0 + 512]),
                    waits=([(sT, xxx_tok)] if xxx_tok else []))
                p.vdrain()
                p.plan("vector", lambda v, cc=cc, t0=t0: v.scalar_tensor_tensor(
                    xxxb[:, :], xxs3[:, 0, :], self.maa_col(0, cc),
                    xnT3[:, cc, t0:t0 + 512], ALU.mult, ALU.add),
                    inc=(sV, 1))
                xv = p.val(sV)
                for q5 in range(5):
                    w = [(sV, xv), (sG, gd)]
                    if cc == 0 and m_ev:
                        w.append((sS, m_ev))
                    p.plan("tensor", lambda t, cc=cc, q5=q5: t.matmul(
                        ps_m[q5][:, :], w1_3[:, cc, 32 * q5:32 * q5 + 32],
                        xxxb[:, :], start=(cc == 0), stop=(cc == NCC - 1)),
                        waits=(w if (cc == 0 or q5 == 0) else ()),
                        inc=((sT, 1) if q5 == 4 else None))
                xxx_tok = p.val(sT)
            tt = p.val(sT)
            for q5 in range(5):
                p.plan("scalar", lambda s, q5=q5, nch=nch: s.activation(
                    m_sb[0:32, q5 * C + nch * 512: q5 * C + (nch + 1) * 512],
                    ps_m[q5][:, :], AF.Tanh),
                    waits=([(sT, tt)] if q5 == 0 else []),
                    inc=((sS, 1) if q5 == 4 else None))
            m_ev = p.val(sS)
        m_done = p.val(sS)

        QL = "wkvrg"
        mix_ev = [None, None]
        mixc_ev = [None, None]
        xx_done = {}
        spill_tok = [0, 0]
        for cc in range(NCC):
            t0 = 128
            p.plan("vector", lambda v, cc=cc: v.tensor_sub(
                xxs[:, :], xnT3[:, cc, 127:127 + TOWN],
                xnT3[:, cc, 128:128 + TOWN]), inc=(sV, 1))
            p.vdrain()
            xx_v = p.val(sV)
            for qi, q in enumerate(QL):
                ab = (cc * 5 + qi) % 2
                for nch in range(4):
                    bk = (qi * 4 + nch) % 2
                    w = [(sS, m_done), (sG, gd)]
                    if mix_ev[bk] is not None:
                        w.append(mix_ev[bk])
                    p.plan("tensor", lambda t, cc=cc, qi=qi, nch=nch, bk=bk:
                           t.matmul(ps_mix[bk][:, :],
                                    w2_sb[0:32, qi * C + cc * 128:
                                          qi * C + cc * 128 + 128],
                                    m_sb[0:32, qi * C + nch * 512:
                                         qi * C + (nch + 1) * 512],
                                    start=True, stop=True),
                           waits=w, inc=(sT, 1))
                    tt = p.val(sT)
                    wsc = [(sT, tt)]
                    if mixc_ev[bk] is not None:
                        wsc.append(mixc_ev[bk])
                    p.plan("scalar", lambda sc, bk=bk: sc.activation(
                        mixc[bk][:, :], ps_mix[bk][:, :], AF.Copy),
                        waits=wsc, inc=(sS, 1))
                    scv = p.val(sS)
                    p.plan("vector", lambda v, bk=bk, cc=cc, qi=qi, nch=nch:
                           v.scalar_tensor_tensor(
                               t1b[:, :], mixc[bk][:, :],
                               self.maa_col(1 + qi, cc),
                               xxs[:, nch * 512:(nch + 1) * 512],
                               ALU.add, ALU.mult),
                           waits=[(sS, scv), (sV, xx_v)] + (
                               [(sD, spill_tok[ab])]
                               if (nch == 0 and spill_tok[ab]) else []))
                    p.vdrain()
                    p.plan("vector", lambda v, ab=ab, cc=cc, nch=nch:
                           v.tensor_add(
                               xqa[ab][:, nch * 512:(nch + 1) * 512],
                               t1b[:, :],
                               xnT3[:, cc, 128 + nch * 512:
                                    128 + (nch + 1) * 512]),
                           inc=(sV, 1))
                    mix_ev[bk] = (sS, scv)
                    mixc_ev[bk] = (sV, p.val(sV))
                vv = p.val(sV)
                p.dma("sync", self.sp_xq[q][cc, :, :], xqa[ab][:, :], sD,
                      waits=[(sV, vv)])
                spill_tok[ab] = p.val(sD)
        xnb = xqa[0]
        xnb_d = spill_tok[0]
        for cc in range(NCC):
            w = [(sD, xnb_d)] if xnb_d else []
            p.plan("scalar", lambda s, cc=cc: s.activation(
                xnb[:, :], xnT3[:, cc, 128:128 + TOWN], AF.Copy),
                waits=w, inc=(sS, 1))
            sv = p.val(sS)
            p.dma("sync", self.sp_xnb[cc, :, :], xnb[:, :], sD,
                  waits=[(sS, sv)])
            xnb_d = p.val(sD)
        es.close()

    def p3_proj(self):
        nc, p = self.nc, self.p
        sD, sG, sT, sV, sS = self.sD, self.sG, self.sT, self.sV, self.sS

        es = contextlib.ExitStack()
        xq_sb = self.sb("xq_sb", (128, NCC * TOWN), BF16, es)
        td1_sb = self.sb("td1_sb", (128, 16 * 64), BF16, es)
        td2_sb = self.sb("td2_sb", (64, C), BF16, es)
        tdt = self.sb("tdt", (64, TOWN), BF16, es)
        ebuf = [self.sb(f"ebuf{i}", (128, TOWN), F32, es) for i in range(2)]
        ps_d = self.ps("p3d", (64, 512), F32, es)
        ps_e = [self.ps(f"p3e{i}", (128, 512), F32, es) for i in range(2)]
        xq3 = r3(xq_sb, TOWN)
        td1_3 = r3(td1_sb, 64)

        p.dma("gpsimd", td1_3[:, :, :],
              self.td1.rearrange("(a p) d -> p a d", p=128), sG)
        gd1 = p.dma("gpsimd", td2_sb[:, :], self.td2, sG)
        ld = p.dma("sync", xq3[:, :, :],
                   self.sp_xq["w"].rearrange("a p t -> p a t"), sD)
        tanh_tok = 0
        for nch in range(4):
            for cc in range(NCC):
                w = [(sG, gd1), (sD, ld)]
                if cc == 0 and nch > 0:
                    w.append((sS, tanh_tok))
                p.plan("tensor", lambda t, cc=cc, nch=nch: t.matmul(
                    ps_d[:, :], td1_3[:, cc, :],
                    xq3[:, cc, nch * 512:(nch + 1) * 512],
                    start=(cc == 0), stop=(cc == NCC - 1)),
                    waits=(w if cc == 0 else ()),
                    inc=((sT, 1) if cc == NCC - 1 else None))
            tt = p.val(sT)
            p.plan("scalar", lambda s, nch=nch: s.activation(
                tdt[:, nch * 512:(nch + 1) * 512], ps_d[:, :], AF.Tanh),
                waits=[(sT, tt)], inc=(sS, 1))
            tanh_tok = p.val(sS)
        e_ev = [0, 0]
        e_dma = [0, 0]
        for cc in range(NCC):
            bi = cc % 2
            for nch in range(4):
                bk = (cc * 4 + nch) % 2
                w = [(sS, tanh_tok)]
                if e_ev[bk]:
                    w.append((sS, e_ev[bk]))
                p.plan("tensor", lambda t, cc=cc, nch=nch, bk=bk: t.matmul(
                    ps_e[bk][:, :], td2_sb[:, cc * 128:(cc + 1) * 128],
                    tdt[:, nch * 512:(nch + 1) * 512], start=True, stop=True),
                    waits=w, inc=(sT, 1))
                tt = p.val(sT)
                w2_ = [(sT, tt)]
                if nch == 0 and e_dma[bi]:
                    w2_.append((sD, e_dma[bi]))
                p.plan("scalar", lambda s, cc=cc, nch=nch, bk=bk, bi=bi:
                       s.activation(ebuf[bi][:, nch * 512:(nch + 1) * 512],
                                    ps_e[bk][:, :], AF.Exp,
                                    bias=self.maa_col(6, cc)),
                       waits=w2_, inc=(sS, 1))
                e_ev[bk] = p.val(sS)
            sv = p.val(sS)
            p.dma("sync", self.sp_e[cc, :, :], ebuf[bi][:, :], sD,
                  waits=[(sS, sv)])
            e_dma[bi] = p.val(sD)
        es.close()
        p.barrier(self.sB)

        for q, fm in (("k", True), ("r", True), ("v", False), ("g", False)):
            es = contextlib.ExitStack()
            xq_sb = self.sb(f"xq_sb_{q}", (128, NCC * TOWN), BF16, es)
            w_sb = self.sb(f"w_sb_{q}", (128, NCC * C), BF16, es)
            ob = [self.sb(f"ob_{q}{i}", (128, C), BF16, es) for i in range(2)]
            psm = [self.ps(f"p3m_{q}{i}", (128, 512), F32, es) for i in range(2)]
            xq3 = r3(xq_sb, TOWN)
            w3 = r3(w_sb, C)
            guard = getattr(self, "_p3_guard", [])
            ld = p.dma("sync", xq3[:, :, :],
                       self.sp_xq[q].rearrange("a p t -> p a t"), sD,
                       waits=guard)
            gd = p.dma("gpsimd", w3[:, :, :],
                       self.wq[q].rearrange("(a p) j -> p a j", p=128), sG,
                       waits=guard)
            ev_tok = [None, None]
            dma_tok = [0, 0]
            spill = {"r": self.sp_r, "k": self.sp_k,
                     "v": self.sp_v, "g": self.sp_g}[q]
            for oc in range(16):
                bi = oc % 2
                for nch in range(4):
                    bk = (oc * 4 + nch) % 2
                    w = [(sD, ld), (sG, gd)]
                    if ev_tok[bk] is not None:
                        w.append(ev_tok[bk])
                    for cc in range(NCC):
                        def mm(t, cc=cc, oc=oc, nch=nch, bk=bk, fm=fm):
                            if fm:
                                lhsT = w3[:, cc, oc * 128:(oc + 1) * 128]
                                rhs = xq3[:, cc, nch * 512:(nch + 1) * 512]
                            else:
                                lhsT = xq3[:, cc, oc * 128:(oc + 1) * 128]
                                rhs = w3[:, cc, nch * 512:(nch + 1) * 512]
                            return t.matmul(psm[bk][:, :], lhsT, rhs,
                                            start=(cc == 0),
                                            stop=(cc == NCC - 1))
                        p.plan("tensor", mm, waits=(w if cc == 0 else ()),
                               inc=((sT, 1) if cc == NCC - 1 else None))
                    tt = p.val(sT)
                    wdma = [(sD, dma_tok[bi])] if (nch == 0 and dma_tok[bi]) \
                        else []
                    if q == "g":
                        p.plan("scalar", lambda s, bi=bi, nch=nch, bk=bk:
                               s.activation(
                                   ob[bi][:, nch * 512:(nch + 1) * 512],
                                   psm[bk][:, :], AF.Silu),
                               waits=[(sT, tt)] + wdma, inc=(sS, 1))
                        ev_tok[bk] = (sS, p.val(sS))
                    else:
                        p.plan("vector", lambda v, bi=bi, nch=nch, bk=bk:
                               v.tensor_copy(
                                   ob[bi][:, nch * 512:(nch + 1) * 512],
                                   psm[bk][:, :]),
                               waits=[(sT, tt)] + wdma, inc=(sV, 1))
                        ev_tok[bk] = (sV, p.val(sV))
                sem = sS if q == "g" else sV
                sv = p.val(sem)
                p.dma("sync", spill[oc, :, :], ob[bi][:, :], sD,
                      waits=[(sem, sv)])
                dma_tok[bi] = p.val(sD)
            es.close()
            self._p3_guard = [(sT, p.val(sT)), (sV, p.val(sV)),
                              (sS, p.val(sS)), (sD, p.val(sD))]
        p.barrier(self.sB)

    def p4_scan(self):
        nc, p = self.nc, self.p
        sD, sG, sT, sV, sS = self.sD, self.sG, self.sT, self.sV, self.sS
        sC = self.sC
        es = contextlib.ExitStack()
        r_ch = [self.sb(f"r_ch{i}", (128, C), BF16, es) for i in range(2)]
        k_ch = [self.sb(f"k_ch{i}", (128, C), BF16, es) for i in range(2)]
        v_ch = [self.sb(f"v_ch{i}", (128, C), BF16, es) for i in range(2)]
        e_ch = [self.sb(f"e_ch{i}", (128, C), F32, es) for i in range(2)]
        cs = [self.sb(f"cs{i}", (128, 16 * 129), F32, es) for i in range(2)]
        csc = [self.sb(f"csc{i}", (128, 16 * 129), F32, es) for i in range(2)]
        Cc = [self.sb(f"Cc{i}", (128, 16), F32, es) for i in range(2)]
        AL = [self.sb(f"AL{i}", (128, 16), F32, es) for i in range(2)]
        sdel = [self.sb(f"sdel{i}", (128, 16), F32, es) for i in range(2)]
        expmC = [self.sb(f"expmC{i}", (128, 16), F32, es) for i in range(2)]
        rpsc = self.sb("rpsc", (128, 16), F32, es)
        Dpref = self.sb("Dpref", (128, 16), F32, es)
        dec_r = [self.sb(f"dec_r{i}", (128, C), BF16, es) for i in range(2)]
        dec_k = [self.sb(f"dec_k{i}", (128, C), BF16, es) for i in range(2)]
        rt = [self.sb(f"rt{i}", (128, C), BF16, es) for i in range(2)]
        kt = [self.sb(f"kt{i}", (128, C), BF16, es) for i in range(2)]
        rk = self.sb("rk", (128, C), BF16, es)
        ruk = [self.sb(f"ruk{i}", (128, C), BF16, es) for i in range(2)]
        kt_tm = self.sb("kt_tm", (128, C), BF16, es)
        ruk_tm = self.sb("ruk_tm", (128, C), BF16, es)
        Dv = self.sb("Dv", (128, 32), F32, es)
        S0b = self.sb("S0b", (128, 1024), BF16, es)
        S0b_ev = self.sb("S0b_ev", (128, 1024), BF16, es)
        S0b_od = self.sb("S0b_od", (128, 1024), BF16, es)
        ptm = [self.sb(f"ptm{i}", (128, 128), BF16, es) for i in range(2)]
        dvb = [self.sb(f"dvb{i}", (128, 512), F32, es) for i in range(2)]
        stmp = self.sb("stmp", (128, 1024), F32, es)
        sdel_x = [self.sb(f"sdel_x{i}", (128, 1024), F32, es) for i in range(2)]
        al_x = [self.sb(f"al_x{i}", (128, 1024), F32, es) for i in range(2)]
        emc_x = [self.sb(f"emc_x{i}", (128, 1024), F32, es) for i in range(2)]
        s_t2 = self.sb("s_t2", (128, 1024), F32, es)
        ytile = [self.sb(f"yt{i}", (128, C), F32, es) for i in range(2)]
        rpb = [self.sb(f"rp{i}", (128, C), BF16, es) for i in range(2)]
        ps_t1 = self.ps("p4t1", (128, 1024), BF16, es)
        ps_t2 = self.ps("p4t2", (128, 1024), BF16, es)
        ps_pt = [self.ps(f"p4pt{i}", (128, 512), F32, es) for i in range(2)]
        ps_y = [self.ps(f"p4y{i}", (128, 512), F32, es) for i in range(2)]
        ps_s = self.ps("p4s", (128, 1024), F32, es)
        self.S_a = self.sb("S_a", (128, 1024), F32, es)
        self.S_b = self.sb("S_b", (128, 1024), F32, es)
        cs3 = [r3(cs[i], 129) for i in range(2)]
        csc3 = [r3(csc[i], 129) for i in range(2)]

        p.plan("vector", lambda v: v.memset(cs[0][:, :], 0.0))
        p.plan("vector", lambda v: v.memset(cs[1][:, :], 0.0))
        p.plan("vector", lambda v: v.memset(self.S_a[:, :], 0.0))
        p.plan("vector", lambda v: v.memset(dvb[0][:, :], 0.0))
        p.plan("vector", lambda v: v.memset(dvb[1][:, :], 0.0))
        p.plan("vector", lambda v: v.memset(ptm[0][:, :], 0.0))
        p.plan("vector", lambda v: v.memset(ptm[1][:, :], 0.0))
        p.plan("vector", lambda v: v.memset(Dpref[:, :], 1.0), inc=(sV, 1))
        init_v = p.val(sV)

        ld_tok = [0, 0]
        endT = [0, 0]
        ydma = [0, 0]
        dec_tok = 0
        prevT = 0
        tr_ev = [None, None]
        tr2_ev = [None, None]
        pt_ev = [None, None]
        ptfree = [0, 0]
        y_ev = [None, None]
        s_ev = init_v

        def chunk_load(ch):
            bi = ch % 2
            w = []
            if ch >= 2:
                w = [(sT, endT[bi]), (sV, p.val(sV))]
            p.dma("sync", r3(r_ch[bi], 128)[:, :, :],
                  self.sp_r.rearrange("a p t -> p a t")
                  [:, :, ch * 128:(ch + 1) * 128], sD, waits=w)
            p.dma("sync", r3(k_ch[bi], 128)[:, :, :],
                  self.sp_k.rearrange("a p t -> p a t")
                  [:, :, ch * 128:(ch + 1) * 128], sD)
            p.dma("sync", r3(e_ch[bi], 128)[:, :, :],
                  self.sp_e.rearrange("a p t -> p a t")
                  [:, :, ch * 128:(ch + 1) * 128], sD)
            p.dma("sync", v_ch[bi][:, :], self.sp_v[ch, :, :], sD)
            ld_tok[bi] = p.val(sD)

        chunk_load(0)
        chunk_load(1)

        dec_tok_p = [0, 0]
        cs_v_p = [0, 0]
        elem_v_p = [0, 0]
        exp_v_p = [0, 0]
        s_ev_hist = [init_v, init_v]
        d_prev = [0]
        bodyT_prev = [0]
        sdel_prev = [0]
        st = {"s_ev": init_v, "s0": 0}

        def prefix_a(ch):
            bi = ch % 2
            for cc in range(NCC):
                w = []
                if cc == 0:
                    w = [(sD, ld_tok[bi])]
                    if dec_tok_p[bi]:
                        w.append((sS, dec_tok_p[bi]))
                p.plan("vector", lambda v, cc=cc, bi=bi: v.tensor_tensor_scan(
                    cs3[bi][:, cc, 1:129], self.ones[:, :],
                    e_ch[bi][:, cc * 128:(cc + 1) * 128], 0.0,
                    ALU.mult, ALU.add), waits=w)
            p.vdrain()
            p.plan("vector", lambda v, bi=bi: v.tensor_scalar(
                Cc[bi][:, :], cs3[bi][:, :, 128:129], 0.5, None, ALU.mult))
            p.vdrain()
            for cc in range(NCC):
                p.plan("vector", lambda v, cc=cc, bi=bi: v.tensor_scalar(
                    csc3[bi][:, cc, 0:129], cs3[bi][:, cc, 0:129],
                    Cc[bi][:, cc:cc + 1], None, ALU.subtract), inc=(sV, 1))
            cs_v_p[bi] = p.val(sV)

        def prefix_b(ch):
            bi = ch % 2
            cs_v = cs_v_p[bi]
            w = [(sV, cs_v)]
            if elem_v_p[bi]:
                w = [(sV, max(cs_v, elem_v_p[bi]))]
            p.plan("scalar", lambda sc, bi=bi: sc.activation(
                dec_r[bi][:, :], csc3[bi][:, :, 0:128], AF.Exp, scale=-1.0),
                waits=w)
            p.plan("scalar", lambda sc, bi=bi: sc.activation(
                dec_k[bi][:, :], csc3[bi][:, :, 1:129], AF.Exp))
            p.plan("scalar", lambda sc, bi=bi: sc.activation(
                AL[bi][:, :], cs3[bi][:, :, 128:129], AF.Exp, scale=-1.0))
            p.plan("scalar", lambda sc, bi=bi: sc.activation(
                sdel[bi][:, :], csc3[bi][:, :, 128:129], AF.Exp, scale=-1.0))
            p.plan("scalar", lambda sc, bi=bi: sc.activation(
                expmC[bi][:, :], Cc[bi][:, :], AF.Exp, scale=-1.0),
                inc=(sS, 1))
            dec_tok_p[bi] = p.val(sS)
            wexp = [(sS, dec_tok_p[bi]), (sV, s_ev_hist[bi])]
            for kk in range(16):
                p.plan("vector", lambda v, kk=kk, bi=bi: v.tensor_scalar(
                    sdel_x[bi][:, kk * 64:(kk + 1) * 64], self.ones[:, 0:64],
                    sdel[bi][:, kk:kk + 1], None, ALU.mult),
                    waits=([(x, y) for x, y in wexp if y] if kk == 0 else ()))
            for kk in range(16):
                p.plan("vector", lambda v, kk=kk, bi=bi: v.tensor_scalar(
                    al_x[bi][:, kk * 64:(kk + 1) * 64], self.ones[:, 0:64],
                    AL[bi][:, kk:kk + 1], None, ALU.mult))
            for kk in range(16):
                p.plan("vector", lambda v, kk=kk, bi=bi: v.tensor_scalar(
                    emc_x[bi][:, kk * 64:(kk + 1) * 64], self.ones[:, 0:64],
                    expmC[bi][:, kk:kk + 1], None, ALU.mult))
            p.vdrain()
            p.plan("vector", None, inc=(sV, 1))
            exp_v_p[bi] = p.val(sV)
            w = [(sS, dec_tok_p[bi]), (sT, endT[bi])]
            p.plan("vector", lambda v, bi=bi: v.tensor_mul(
                rt[bi][:, :], r_ch[bi][:, :], dec_r[bi][:, :]),
                waits=[(x, y) for x, y in w if y])
            p.plan("vector", lambda v, bi=bi: v.tensor_mul(
                kt[bi][:, :], k_ch[bi][:, :], dec_k[bi][:, :]))
            p.plan("vector", lambda v, bi=bi: v.tensor_mul(
                rk[:, :], r_ch[bi][:, :], k_ch[bi][:, :]))
            p.vdrain()
            for cc in range(NCC):
                p.plan("vector", lambda v, cc=cc, bi=bi: v.tensor_scalar(
                    ruk[bi][:, cc * 128:(cc + 1) * 128],
                    rk[:, cc * 128:(cc + 1) * 128],
                    self.u_sb[:, cc:cc + 1], None, ALU.mult))
            p.plan("vector", lambda v, bi=bi: v.tensor_mul(
                rpsc[:, :], expmC[bi][:, :], Dpref[:, :]))
            p.vdrain()
            rp_w = [(sD, ydma[bi])] if ydma[bi] else []
            for cc in range(NCC):
                p.plan("vector", lambda v, cc=cc, bi=bi: v.tensor_scalar(
                    rpb[bi][:, cc * 128:(cc + 1) * 128],
                    rt[bi][:, cc * 128:(cc + 1) * 128],
                    rpsc[:, cc:cc + 1], None, ALU.mult),
                    waits=(rp_w if cc == 0 else ()))
            p.plan("vector", lambda v, bi=bi: v.tensor_mul(
                Dpref[:, :], Dpref[:, :], AL[bi][:, :]), inc=(sV, 1))
            elem_v_p[bi] = p.val(sV)

        def body(ch):
            bi = ch % 2
            S_cur = self.S_a if ch % 2 == 0 else self.S_b
            S_nxt = self.S_b if ch % 2 == 0 else self.S_a
            elem_v = elem_v_p[bi]
            for half in range(2):
                for j in range(8):
                    cc = half * 8 + j
                    w = [(sV, elem_v)]
                    if j == 0 and tr_ev[0] is not None:
                        w.append(tr_ev[0])
                    p.plan("tensor", lambda t, cc=cc, j=j, bi=bi: t.transpose(
                        ps_t1[:, j * 128:(j + 1) * 128],
                        kt[bi][:, cc * 128:(cc + 1) * 128], self.idbf[:, :]),
                        waits=w, inc=((sT, 1) if j == 7 else None))
                tt = p.val(sT)
                w = [(sT, tt)]
                if sdel_prev[0]:
                    w.append((sT, sdel_prev[0]))
                p.plan("vector", lambda v, half=half: v.tensor_copy(
                    kt_tm[:, half * 1024:(half + 1) * 1024], ps_t1[:, :]),
                    waits=w, inc=(sV, 1))
                tr_ev[0] = (sV, p.val(sV))
            for half in range(2):
                for j in range(8):
                    cc = half * 8 + j
                    w = [(sV, elem_v)]
                    if j == 0 and tr2_ev[0] is not None:
                        w.append(tr2_ev[0])
                    p.plan("tensor", lambda t, cc=cc, j=j, bi=bi: t.transpose(
                        ps_t2[:, j * 128:(j + 1) * 128],
                        ruk[bi][:, cc * 128:(cc + 1) * 128], self.idbf[:, :]),
                        waits=w, inc=((sT, 1) if j == 7 else None))
                tt = p.val(sT)
                w = [(sT, tt)]
                if d_prev[0]:
                    w.append((sV, d_prev[0]))
                p.plan("scalar", lambda sc, half=half: sc.activation(
                    ruk_tm[:, half * 1024:(half + 1) * 1024], ps_t2[:, :],
                    AF.Copy), waits=w, inc=(sS, 1))
                tr2_ev[0] = (sS, p.val(sS))
            ktm_v = tr_ev[0][1]
            ruk_s = tr2_ev[0][1]
            sdel_grp = [0, 0]
            for h in range(H):
                po = 64 * (h % 2)
                pso = ((h // 2) % 8) * 64 + (512 if h >= 16 else 0)
                w = [(sV, ktm_v)]
                if h == 0 and st["s_ev"]:
                    w.append((sV, st["s_ev"]))
                p.plan("tensor", lambda t, h=h, bi=bi, po=po, pso=pso: t.matmul(
                    ps_s[po:po + 64, pso:pso + 64],
                    kt_tm[:, h * 64:(h + 1) * 64],
                    v_ch[bi][:, h * 64:(h + 1) * 64],
                    start=(h in (0, 1, 16, 17)), stop=True),
                    waits=w, inc=((sT, 1) if h in (15, 31) else None))
                if h == 15:
                    sdel_grp[0] = p.val(sT)
                if h == 31:
                    sdel_grp[1] = p.val(sT)
            sdel_prev[0] = sdel_grp[1]
            for h in range(H):
                p.plan("vector", lambda v, h=h: v.reduce_sum(
                    Dv[:, h:h + 1], ruk_tm[:, h * 64:(h + 1) * 64], axis=AX),
                    waits=([(sS, ruk_s)] if h == 0 else ()),
                    inc=((sV, 1) if h == H - 1 else None))
            d_tok = p.val(sV)
            d_prev[0] = d_tok
            w = [(sV, exp_v_p[bi]), (sV, st["s_ev"])]
            if bodyT_prev[0]:
                w.append((sT, bodyT_prev[0]))
            p.plan("vector", lambda v, S_cur=S_cur, bi=bi: v.tensor_mul(
                S0b[:, :], S_cur[:, :], emc_x[bi][:, :]),
                waits=[(x, y) for x, y in w if y])
            p.vdrain()
            p.plan("vector", lambda v: v.tensor_scalar(
                S0b_ev[:, :], S0b[:, :], self.mrow[:, 0:1], None, ALU.mult))
            p.plan("vector", lambda v: v.tensor_scalar(
                S0b_od[:, :], S0b[:, :], self.mrow[:, 1:2], None, ALU.mult),
                inc=(sV, 1))
            s0_tok = p.val(sV)
            p.plan("vector", lambda v, bi=bi: v.tensor_mul(
                stmp[:, 0:512], ps_s[:, 0:512], sdel_x[bi][:, 0:512]),
                waits=[(sT, sdel_grp[0]), (sV, exp_v_p[bi])])
            p.plan("vector", lambda v, bi=bi: v.tensor_mul(
                stmp[:, 512:1024], ps_s[:, 512:1024], sdel_x[bi][:, 512:1024]),
                waits=[(sT, sdel_grp[1])])
            p.plan("vector", lambda v, S_cur=S_cur, bi=bi: v.tensor_mul(
                s_t2[:, :], S_cur[:, :], al_x[bi][:, :]))
            p.vdrain()
            p.plan("vector", lambda v, S_nxt=S_nxt: v.tensor_add(
                S_nxt[:, :], s_t2[:, :], stmp[:, :]), inc=(sV, 1))
            st["s_ev"] = p.val(sV)
            s_ev_hist[bi] = st["s_ev"]
            ygrp_tok = [0] * 4
            for h in range(H):
                po = 64 * (h % 2)
                cc = h // 2
                hs = slice(po, po + 64)
                fs = slice(cc * 128, cc * 128 + 128)
                bk = h % 2
                grp = h // 8
                yb = grp % 2
                w = [(sV, elem_v)]
                if pt_ev[bk] is not None:
                    w.append(pt_ev[bk])
                p.plan("tensor", lambda t, hs=hs, fs=fs, bk=bk, bi=bi: t.matmul(
                    ps_pt[bk][:, 0:128], kt[bi][hs, fs], rt[bi][hs, fs],
                    start=True, stop=True), waits=w, inc=(sT, 1))
                tt = p.val(sT)
                w = [(sT, tt)]
                if ptfree[bk]:
                    w.append((sT, ptfree[bk]))
                p.plan("vector", lambda v, bk=bk: v.copy_predicated(
                    ptm[bk][:, :], self.masksu[:, :], ps_pt[bk][:, 0:128]),
                    waits=w, inc=(sV, 1))
                pt_ev[bk] = (sV, p.val(sV))
                w = [pt_ev[bk]]
                if h % 8 == 0:
                    if y_ev[yb] is not None:
                        w.append(y_ev[yb])
                    w.append((sV, s0_tok))
                p.plan("tensor", lambda t, h=h, yb=yb, bk=bk, bi=bi: t.matmul(
                    ps_y[yb][:, (h % 8) * 64:(h % 8) * 64 + 64],
                    ptm[bk][:, :], v_ch[bi][:, h * 64:(h + 1) * 64],
                    start=(h % 8 == 0), stop=False), waits=w)
                p.plan("tensor", lambda t, h=h, yb=yb, fs=fs, bi=bi: t.matmul(
                    ps_y[yb][:, (h % 8) * 64:(h % 8) * 64 + 64],
                    rt[bi][:, fs],
                    (S0b_ev if h % 2 == 0 else S0b_od)
                    [:, 64 * (h // 2):64 * (h // 2) + 64],
                    start=False, stop=True), inc=(sT, 1))
                ptfree[bk] = p.val(sT)
                if h % 8 == 7:
                    ygrp_tok[grp] = p.val(sT)
                w = [(sV, d_tok)]
                if h % 8 == 0 and y_ev[yb] is not None:
                    w.append(y_ev[yb])
                p.plan("vector", lambda v, h=h, yb=yb, bi=bi: v.tensor_scalar(
                    dvb[yb][:, (h % 8) * 64:(h % 8) * 64 + 64],
                    v_ch[bi][:, h * 64:(h + 1) * 64],
                    Dv[:, h:h + 1], None, ALU.mult), waits=w)
                if h % 8 == 7:
                    vv = p.val(sV)
                    w = [(sT, ygrp_tok[grp]), (sV, vv)]
                    if grp == 0 and ydma[bi]:
                        w.append((sD, ydma[bi]))
                    p.plan("vector", lambda v, grp=grp, yb=yb, bi=bi:
                           v.tensor_add(
                               ytile[bi][:, grp * 512:(grp + 1) * 512],
                               ps_y[yb][:, :], dvb[yb][:, :]),
                           waits=w, inc=(sV, 1))
                    y_ev[yb] = (sV, p.val(sV))
            yv = max(y_ev[0][1], y_ev[1][1])
            p.dma("sync", self.sp_y[ch, :, :], ytile[bi][:, :], sD,
                  waits=[(sV, yv)])
            p.dma("sync", self.sp_rp[ch, :, :], rpb[bi][:, :], sD)
            ydma[bi] = p.val(sD)
            endT[bi] = p.val(sT)

        prefix_a(0)
        prefix_b(0)
        prefix_a(1)
        prefix_b(1)
        for ch in range(NCH):
            body(ch)
            bodyT_prev[0] = p.val(sT)
            if ch + 2 < NCH:
                chunk_load(ch + 2)
                prefix_a(ch + 2)
            if ch + 1 < NCH:
                pass
            if ch + 2 < NCH:
                prefix_b(ch + 2)
        s_ev = st["s_ev"]

        S_fin = self.S_a
        p.plan("vector", lambda v: v.tensor_scalar(
            csc[0][:, 0:1024], S_fin[:, :], self.cmask_sb[:, 0:1], None,
            ALU.mult), waits=[(sV, s_ev)], inc=(sV, 1))
        vv = p.val(sV)
        p.dma("sync", self.cc_in[:, :], csc[0][:, 0:1024], sD,
              waits=[(sV, vv)])
        p.dma("sync", self.sp_sown[:, :], S_fin[:, :], sD)
        dd = p.val(sD)
        if USE_CC:
            p.plan("gpsimd", lambda g: g.collective_compute(
                "AllReduce", ALU.add,
                replica_groups=[[0, 1], [2, 3], [4, 5], [6, 7]],
                ins=[self.cc_in], outs=[self.cc_out]),
                waits=[(sD, dd)], inc=(sC, 1))
        else:
            p.plan("gpsimd", None, waits=[(sD, dd)], inc=(sC, 16))
        es.close()

    def p5_ffn1(self):
        nc, p = self.nc, self.p
        sD, sG, sT, sV, sS = self.sD, self.sG, self.sT, self.sV, self.sS
        es = contextlib.ExitStack()
        xnb_sb = self.sb("xnb_sb", (128, NCC * TOWN), BF16, es)
        wgb = [self.sb(f"wgb{i}", (128, 16 * 512), BF16, es) for i in range(2)]
        wub = [self.sb(f"wub{i}", (128, 16 * 512), BF16, es) for i in range(2)]
        sg_sb = self.sb("sg_sb", (128, 64), F32, es)
        su_sb = self.sb("su_sb", (128, 64), F32, es)
        gbuf = [self.sb(f"gb{i}", (128, 512), BF16, es) for i in range(2)]
        ubuf = [self.sb(f"ub{i}", (128, 512), BF16, es) for i in range(2)]
        fb = [self.sb(f"fb{i}", (128, TOWN), BF16, es) for i in range(2)]
        ps_g = [self.ps(f"p5g{i}", (128, 512), F32, es) for i in range(2)]
        ps_u = [self.ps(f"p5u{i}", (128, 512), F32, es) for i in range(2)]
        xnb3 = r3(xnb_sb, TOWN)

        p.dma("sync", xnb3[:, :, :],
              self.sp_xnb.rearrange("a p t -> p a t"), sD)
        p.dma("sync", sg_sb[:, :], self.sg_in, sD)
        ld = p.dma("sync", su_sb[:, :], self.su_in, sD)
        wtok = [0, 0]
        fdma = [0, 0]
        ev_g = [None, None]
        ev_u = [None, None]
        fuse_tok = [0, 0]
        fuse_v = 0
        for os_ in range(16):
            bi = os_ % 2
            w = [(sT, wtok[bi])] if wtok[bi] else []
            p.dma("gpsimd", r3(wgb[bi], 512)[:, :, :],
                  self.wgate[os_].rearrange("(a p) o -> p a o", p=128), sG,
                  waits=w)
            gd = p.dma("gpsimd", r3(wub[bi], 512)[:, :, :],
                       self.wup[os_].rearrange("(a p) o -> p a o", p=128), sG)
            for o128 in range(4):
                oc = os_ * 4 + o128
                fbi = oc % 2
                for tN in range(4):
                    bk = (o128 * 4 + tN) % 2
                    w = [(sG, gd), (sD, ld)]
                    if ev_g[bk] is not None:
                        w.append(ev_g[bk])
                    for cc in range(NCC):
                        p.plan("tensor", lambda t, cc=cc, o128=o128, tN=tN,
                               bk=bk, bi=bi: t.matmul(
                                   ps_g[bk][:, :],
                                   r3(wgb[bi], 512)
                                   [:, cc, o128 * 128:(o128 + 1) * 128],
                                   xnb3[:, cc, tN * 512:(tN + 1) * 512],
                                   start=(cc == 0), stop=(cc == NCC - 1)),
                               waits=(w if cc == 0 else ()))
                    w = []
                    if ev_u[bk] is not None:
                        w.append(ev_u[bk])
                    for cc in range(NCC):
                        p.plan("tensor", lambda t, cc=cc, o128=o128, tN=tN,
                               bk=bk, bi=bi: t.matmul(
                                   ps_u[bk][:, :],
                                   r3(wub[bi], 512)
                                   [:, cc, o128 * 128:(o128 + 1) * 128],
                                   xnb3[:, cc, tN * 512:(tN + 1) * 512],
                                   start=(cc == 0), stop=(cc == NCC - 1)),
                               waits=(w if cc == 0 else ()),
                               inc=((sT, 1) if cc == NCC - 1 else None))
                    tt = p.val(sT)
                    wf = [(sT, tt)]
                    if fuse_tok[bk]:
                        wf.append((sV, fuse_tok[bk]))
                    p.plan("scalar", lambda s, oc=oc, bk=bk: s.activation(
                        gbuf[bk][:, :], ps_g[bk][:, :], AF.Silu,
                        scale=sg_sb[:, oc:oc + 1]), waits=wf, inc=(sS, 1))
                    ev_g[bk] = (sS, p.val(sS))
                    p.plan("scalar", lambda s, oc=oc, bk=bk: s.activation(
                        ubuf[bk][:, :], ps_u[bk][:, :], AF.Copy,
                        scale=su_sb[:, oc:oc + 1]), inc=(sS, 1))
                    ev_u[bk] = (sS, p.val(sS))
                    sv = p.val(sS)
                    wv = [(sS, sv)]
                    if tN == 0 and fdma[fbi]:
                        wv.append((sD, fdma[fbi]))
                    p.plan("vector", lambda v, tN=tN, fbi=fbi, bk=bk:
                           v.tensor_mul(fb[fbi][:, tN * 512:(tN + 1) * 512],
                                        gbuf[bk][:, :], ubuf[bk][:, :]),
                           waits=wv, inc=(sV, 1))
                    fuse_tok[bk] = p.val(sV)
                    fuse_v = p.val(sV)
                p.dma("sync", self.sp_fuse[oc, :, :], fb[fbi][:, :], sD,
                      waits=[(sV, fuse_v)])
                fdma[fbi] = p.val(sD)
            wtok[bi] = p.val(sT)
        es.close()

    def p6_gn(self):
        nc, p = self.nc, self.p
        sD, sG, sT, sV, sS = self.sD, self.sG, self.sT, self.sV, self.sS
        sC = self.sC
        es = contextlib.ExitStack()
        wo_sb = self.sb("wo_sb", (128, NCC * C), BF16, es)
        lnw_sb = self.sb("lnw_sb", (128, C), F32, es)
        lnb_sb = self.sb("lnb_sb", (128, C), F32, es)
        sir = self.sb("sir", (128, 1024), F32, es)
        ysb = [self.sb(f"ysb{i}", (128, C), F32, es) for i in range(2)]
        rp_sb = [self.sb(f"rps{i}", (128, C), BF16, es) for i in range(2)]
        g_sb = [self.sb(f"gsb{i}", (128, C), BF16, es) for i in range(2)]
        x_sb = [self.sb(f"xsb{i}", (128, C), F32, es) for i in range(2)]
        yf = self.sb("yf", (128, C), F32, es)
        yn = self.sb("yn", (128, C), F32, es)
        yg = self.sb("yg", (128, C), BF16, es)
        ygT = self.sb("ygT", (128, C), BF16, es)
        gst = self.sb("gst", (128, 8 * 32), F32, es)
        partb = [self.sb(f"pb{i}", (128, C), F32, es) for i in range(2)]
        ps_c = [self.ps(f"p6c{i}", (128, 512), F32, es) for i in range(4)]
        ps_t = self.ps("p6t", (128, 1024), BF16, es)
        self.Sinit_b = self.sb("Sinit_b", (128, 1024), BF16, es)
        si_ev = self.sb("si_ev", (128, 1024), BF16, es)
        si_od = self.sb("si_od", (128, 1024), BF16, es)
        wo3 = r3(wo_sb, C)

        gd = p.dma("gpsimd", wo3[:, :, :],
                   self.w_o.rearrange("(a p) j -> p a j", p=128), sG)
        p.dma("sync", lnw_sb[:, :], self.lnw_in, sD)
        dln = p.dma("sync", lnb_sb[:, :], self.lnb_in, sD)
        cw = p.val(sC)
        if SINIT_ZERO:
            p.plan("vector", lambda v: v.memset(self.Sinit_b[:, :], 0.0))
            p.plan("vector", lambda v: v.memset(si_ev[:, :], 0.0))
            p.plan("vector", lambda v: v.memset(si_od[:, :], 0.0),
                   inc=(sV, 1))
            p.plan("scalar", None, waits=[(sV, p.val(sV))], inc=(sS, 1))
        else:
            gd2 = p.dma("gpsimd", sir[:, :], self.cc_out, sG, waits=[(sC, cw)])
            p.plan("vector", lambda v: v.tensor_scalar(
                sir[:, :], sir[:, :], self.imask_sb[:, 0:1], None, ALU.mult),
                waits=[(sG, gd2)], inc=(sV, 1))
            vv = p.val(sV)
            p.plan("scalar", lambda s: s.activation(
                self.Sinit_b[:, :], sir[:, :], AF.Copy), waits=[(sV, vv)],
                inc=(sS, 1))
        sv0 = p.val(sS)
        p.sdrain()
        p.plan("vector", lambda v: v.memset(si_ev[:, :], 0.0))
        p.plan("vector", lambda v: v.memset(si_od[:, :], 0.0), inc=(sV, 1))
        vv0 = p.val(sV)
        p.plan("scalar", lambda s: s.activation(
            si_ev[0:64, :], self.Sinit_b[0:64, :], AF.Copy),
            waits=[(sV, vv0)])
        p.plan("scalar", lambda s: s.activation(
            si_od[64:128, :], self.Sinit_b[64:128, :], AF.Copy),
            inc=(sS, 1))
        si_tok = p.val(sS)

        ld_tok = [0, 0]
        endT = [0, 0]
        pdma = [0, 0]
        apply_prev = 0
        wo_prevT = 0

        def tc_load(tc):
            bi = tc % 2
            w = [(sT, endT[bi]), (sV, p.val(sV))] if tc >= 2 else []
            p.dma("sync", ysb[bi][:, :], self.sp_y[tc, :, :], sD, waits=w)
            p.dma("sync", rp_sb[bi][:, :], self.sp_rp[tc, :, :], sD)
            p.dma("sync", g_sb[bi][:, :], self.sp_g[tc, :, :], sD)
            p.dma("sync", x_sb[bi][:, :],
                  self.x_ext[128 + tc * 128:128 + (tc + 1) * 128, :], sD)
            ld_tok[bi] = p.val(sD)

        tc_load(0)
        tc_load(1)
        corr_ev = [None] * 4
        tr_ev = [None, None]
        wops_tok = [None] * 4
        for tc in range(NTC):
            bi = tc % 2
            grp_tok = [0] * 4
            if 'corr' not in P6_PARTS:
                continue
            for h in range(H):
                po = 64 * (h % 2)
                cc = h // 2
                bk = h // 8
                w = []
                if h % 8 == 0:
                    w = [(sD, ld_tok[bi]), (sS, si_tok), (sG, gd)]
                    if corr_ev[bk] is not None:
                        w.append(corr_ev[bk])
                    if wops_tok[bk] is not None:
                        w.append(wops_tok[bk])
                if 'mm' in P6_PARTS:
                    p.plan("tensor", lambda t, h=h, po=po, cc=cc, bk=bk, bi=bi:
                           t.matmul(ps_c[bk][:, (h % 8) * 64:(h % 8) * 64 + 64],
                                    rp_sb[bi][:, cc * 128:(cc + 1) * 128],
                                    (si_ev if h % 2 == 0 else si_od)
                                    [:, 64 * (h // 2):64 * (h // 2) + 64],
                                    start=(h % 8 == 0), stop=True),
                           waits=w, inc=((sT, 1) if h % 8 == 7 else None))
                if h % 8 == 7:
                    grp_tok[h // 8] = p.val(sT)
            if 'yf' not in P6_PARTS:
                continue
            for i in range(4):
                w = [(sT, grp_tok[i])]
                if apply_prev and i == 0:
                    w.append((sS, apply_prev))
                p.plan("vector", lambda v, i=i, bi=bi: v.tensor_add(
                    yf[:, i * 512:(i + 1) * 512],
                    ysb[bi][:, i * 512:(i + 1) * 512],
                    ps_c[i][:, :] if 'mm' in P6_PARTS else ysb[bi][:, i * 512:(i + 1) * 512]),
                    waits=w, inc=(sV, 1))
                corr_ev[i] = (sV, p.val(sV))
            if 'gn' not in P6_PARTS:
                continue
            p.vdrain()
            p.plan("vector", lambda v: v.tensor_mul(
                yn[:, :], yf[:, :], yf[:, :]))
            p.vdrain()
            for h in range(H):
                p.plan("vector", lambda v, h=h: v.reduce_sum(
                    gst[:, h:h + 1], yf[:, h * 64:(h + 1) * 64], axis=AX))
                p.plan("vector", lambda v, h=h: v.reduce_sum(
                    gst[:, 32 + h:33 + h], yn[:, h * 64:(h + 1) * 64],
                    axis=AX))
            p.vdrain()
            p.plan("vector", lambda v: v.tensor_scalar(
                gst[:, 64:96], gst[:, 0:32], 1.0 / HN, None, ALU.mult))
            p.plan("vector", lambda v: v.tensor_scalar(
                gst[:, 96:128], gst[:, 32:64], 1.0 / HN, None, ALU.mult))
            p.vdrain()
            p.plan("vector", lambda v: v.tensor_mul(
                gst[:, 128:160], gst[:, 64:96], gst[:, 64:96]))
            p.vdrain()
            p.plan("vector", lambda v: v.tensor_sub(
                gst[:, 160:192], gst[:, 96:128], gst[:, 128:160]))
            p.vdrain()
            p.plan("vector", lambda v: v.tensor_scalar(
                gst[:, 160:192], gst[:, 160:192], EPS_GN, None, ALU.add),
                inc=(sV, 1))
            vv = p.val(sV)
            p.plan("scalar", lambda s: s.activation(
                gst[:, 192:224], gst[:, 160:192], AF.Sqrt),
                waits=[(sV, vv)], inc=(sS, 1))
            sv = p.val(sS)
            p.plan("vector", lambda v: v.reciprocal(
                gst[:, 224:256], gst[:, 192:224]), waits=[(sS, sv)])
            p.vdrain()
            p.plan("vector", lambda v: v.tensor_mul(
                gst[:, 128:160], gst[:, 64:96], gst[:, 224:256]))
            p.vdrain()
            p.plan("vector", lambda v: v.tensor_scalar(
                gst[:, 128:160], gst[:, 128:160], -1.0, None, ALU.mult))
            p.vdrain()
            for h in range(H):
                p.plan("vector", lambda v, h=h: v.tensor_scalar(
                    yn[:, h * 64:(h + 1) * 64], yf[:, h * 64:(h + 1) * 64],
                    gst[:, 224 + h:225 + h], gst[:, 128 + h:129 + h],
                    ALU.mult, ALU.add),
                    inc=((sV, 1) if h == H - 1 else None))
            apply_prev = 0
            p.vdrain()
            p.plan("vector", lambda v: v.tensor_mul(
                yn[:, :], yn[:, :], lnw_sb[:, :]), waits=[(sD, dln)])
            p.vdrain()
            p.plan("vector", lambda v: v.tensor_add(
                yn[:, :], yn[:, :], lnb_sb[:, :]))
            p.vdrain()
            p.plan("vector", lambda v, bi=bi: v.tensor_mul(
                yg[:, :], yn[:, :], g_sb[bi][:, :]), inc=(sV, 1))
            yg_v = p.val(sV)
            if 'tr' not in P6_PARTS:
                continue
            for half in range(2):
                for j in range(8):
                    cc = half * 8 + j
                    w = [(sV, yg_v)]
                    if j == 0 and tr_ev[0] is not None:
                        w.append(tr_ev[0])
                    p.plan("tensor", lambda t, cc=cc, j=j: t.transpose(
                        ps_t[:, j * 128:(j + 1) * 128],
                        yg[:, cc * 128:(cc + 1) * 128], self.idbf[:, :]),
                        waits=w, inc=((sT, 1) if j == 7 else None))
                tt = p.val(sT)
                w = [(sT, tt)]
                if wo_prevT:
                    w.append((sT, wo_prevT))
                p.plan("vector", lambda v, half=half: v.tensor_copy(
                    ygT[:, half * 1024:(half + 1) * 1024], ps_t[:, :]),
                    waits=w, inc=(sV, 1))
                tr_ev[0] = (sV, p.val(sV))
            ygT_v = tr_ev[0][1]
            if 'wo' not in P6_PARTS:
                continue
            for cN in range(4):
                w = [(sV, ygT_v)]
                if wops_tok[cN] is not None:
                    w.append(wops_tok[cN])
                w.append(corr_ev[cN])
                for jc in range(NCC):
                    p.plan("tensor", lambda t, jc=jc, cN=cN: t.matmul(
                        ps_c[cN][:, :], ygT[:, jc * 128:(jc + 1) * 128],
                        wo3[:, jc, cN * 512:(cN + 1) * 512],
                        start=(jc == 0), stop=(jc == NCC - 1)),
                        waits=(w if jc == 0 else ()),
                        inc=((sT, 1) if jc == NCC - 1 else None))
                tt = p.val(sT)
                wd = [(sT, tt)]
                if pdma[bi] and cN == 0:
                    wd.append((sD, pdma[bi]))
                p.plan("vector", lambda v, cN=cN, bi=bi: v.tensor_add(
                    partb[bi][:, cN * 512:(cN + 1) * 512], ps_c[cN][:, :],
                    x_sb[bi][:, cN * 512:(cN + 1) * 512]),
                    waits=wd, inc=(sV, 1))
                wops_tok[cN] = (sV, p.val(sV))
            wo_prevT = p.val(sT)
            vv = p.val(sV)
            p.dma("sync", self.sp_part[tc, :, :], partb[bi][:, :], sD,
                  waits=[(sV, vv)])
            pdma[bi] = p.val(sD)
            endT[bi] = p.val(sT)
            if tc + 2 < NTC:
                tc_load(tc + 2)
        es.close()

    def p7_ffn2(self):
        nc, p = self.nc, self.p
        sD, sG, sT, sV, sS = self.sD, self.sG, self.sT, self.sV, self.sS
        es = contextlib.ExitStack()
        fuse_sb = self.sb("fuse_sb", (128, NOC * 1024), BF16, es)
        sdn_sb = self.sb("sdn_sb", (128, C), F32, es)
        wdb = [self.sb(f"wdb{i}", (128, 2 * 512), BF16, es) for i in range(3)]
        pp = [self.sb(f"pp{i}", (128, 512), F32, es) for i in range(2)]
        ob = [self.sb(f"ob{i}", (128, 512), F32, es) for i in range(2)]
        psb = [self.ps(f"p7_{i}", (128, 512), F32, es) for i in range(8)]
        fuse3 = r3(fuse_sb, 1024)

        p.dma("sync", sdn_sb[:, :], self.sdown_in, sD)
        out_d = 0
        ob_d = [0, 0]
        ev_tok = [None] * 8
        mm_tok = [0, 0, 0]
        for tg in range(2):
            w = [(sT, p.val(sT))] if tg == 1 else []
            for i in range(8):
                p.dma("sync", fuse3[:, i * 8:(i + 1) * 8, :],
                      self.sp_fuse.rearrange("a p t -> p a t")
                      [:, i * 8:(i + 1) * 8, tg * 1024:(tg + 1) * 1024],
                      sD, waits=(w if i == 0 else ()))
            fld = p.val(sD)
            for cq in range(4):
                for Kp in range(32):
                    wi = Kp % 3
                    w = [(sT, mm_tok[wi])] if mm_tok[wi] else []
                    gdw = p.dma(
                        "gpsimd", r3(wdb[wi], 512)[:, :, :],
                        self.wdown.rearrange("(a p) c -> p a c", p=128)
                        [:, 2 * Kp:2 * Kp + 2, cq * 512:(cq + 1) * 512],
                        sG, waits=w)
                    for sub in range(2):
                        Ki = 2 * Kp + sub
                        for t8 in range(8):
                            w = []
                            if t8 == 0 and sub == 0:
                                w = [(sG, gdw), (sD, fld)]
                            if Ki == 0 and ev_tok[t8] is not None:
                                w.append(ev_tok[t8])
                            p.plan("tensor", lambda t, Ki=Ki, t8=t8, wi=wi,
                                   sub=sub: t.matmul(
                                       psb[t8][:, :],
                                       fuse3[:, Ki, t8 * 128:(t8 + 1) * 128],
                                       r3(wdb[wi], 512)[:, sub, :],
                                       start=(Ki == 0), stop=(Ki == NOC - 1)),
                                   waits=w,
                                   inc=((sT, 1) if (t8 == 7 and sub == 1)
                                        else None))
                    mm_tok[wi] = p.val(sT)
                kdone = p.val(sT)
                for t8 in range(8):
                    tc = tg * 8 + t8
                    bi = t8 % 2
                    w = [(sD, out_d)] if out_d else []
                    pl = p.dma("sync", pp[bi][:, :],
                               self.sp_part[tc, :, cq * 512:(cq + 1) * 512],
                               sD, waits=w)
                    w = [(sT, kdone), (sD, pl)]
                    if ob_d[bi]:
                        w.append((sD, ob_d[bi]))
                    p.plan("vector", lambda v, t8=t8, bi=bi, cq=cq:
                           v.tensor_mul(ob[bi][:, :], psb[t8][:, :],
                                        sdn_sb[:, cq * 512:(cq + 1) * 512]),
                           waits=w)
                    p.vdrain()
                    p.plan("vector", lambda v, bi=bi: v.tensor_add(
                        ob[bi][:, :], ob[bi][:, :], pp[bi][:, :]),
                        inc=(sV, 1))
                    vv = p.val(sV)
                    p.dma("sync",
                          self.out[tc * 128:(tc + 1) * 128,
                                   cq * 512:(cq + 1) * 512],
                          ob[bi][:, :], sD, waits=[(sV, vv)])
                    ev_tok[t8] = (sV, vv)
                    out_d = p.val(sD)
                    ob_d[bi] = out_d
        p.plan("sync", None, waits=[(sD, p.val(sD))])
        es.close()


_NC_CACHE = {}


def _get_nc(taps=(), stop_after=99):
    key = (tuple(sorted(taps)), stop_after)
    if key not in _NC_CACHE:
        _NC_CACHE[key] = Builder(key[0], stop_after).build()
    return _NC_CACHE[key]


def _prep_inputs(inputs):
    x = np.ascontiguousarray(np.asarray(inputs["L0_attn_x"], np.float32))
    f32 = np.float32

    def pcol(a):
        return np.ascontiguousarray(np.asarray(a, f32).reshape(16, 128).T)

    maa_np = np.stack([
        pcol(np.asarray(inputs["L0_attn_time_maa_x"]).reshape(-1)),
        pcol(np.asarray(inputs["L0_attn_time_maa_w"]).reshape(-1)),
        pcol(np.asarray(inputs["L0_attn_time_maa_k"]).reshape(-1)),
        pcol(np.asarray(inputs["L0_attn_time_maa_v"]).reshape(-1)),
        pcol(np.asarray(inputs["L0_attn_time_maa_r"]).reshape(-1)),
        pcol(np.asarray(inputs["L0_attn_time_maa_g"]).reshape(-1)),
        pcol(np.asarray(inputs["L0_attn_time_decay"]).reshape(-1)),
    ], axis=-1).reshape(128, 16 * 7).astype(f32)
    maa_np = np.ascontiguousarray(maa_np)

    w2 = np.asarray(inputs["L0_attn_w2"], f32)
    w2r = np.ascontiguousarray(w2.transpose(1, 0, 2).reshape(32, 5 * C))
    u_np = pcol(np.asarray(inputs["L0_attn_time_faaaa"]).reshape(-1))
    bcast = lambda a: np.ascontiguousarray(np.broadcast_to(
        np.asarray(a, f32).reshape(1, C), (128, C)))
    scol = lambda a: np.ascontiguousarray(
        np.asarray(a, f32).reshape(64, 128).T)
    common = {
        "w_r": np.ascontiguousarray(np.asarray(inputs["L0_attn_wr"], f32)),
        "w_k": np.ascontiguousarray(np.asarray(inputs["L0_attn_wk"], f32)),
        "w_v": np.ascontiguousarray(np.asarray(inputs["L0_attn_wv"], f32)),
        "w_g": np.ascontiguousarray(np.asarray(inputs["L0_attn_wg"], f32)),
        "w_o": np.ascontiguousarray(np.asarray(inputs["L0_attn_wo"], f32)),
        "w1": np.ascontiguousarray(np.asarray(inputs["L0_attn_w1"], f32)),
        "w2r": w2r,
        "td1": np.ascontiguousarray(np.asarray(inputs["L0_attn_td_w1"], f32)),
        "td2": np.ascontiguousarray(np.asarray(inputs["L0_attn_td_w2"], f32)),
        "maa": maa_np,
        "u_in": u_np,
        "lnw_bc": bcast(inputs["L0_attn_ln_w"]),
        "lnb_bc": bcast(inputs["L0_attn_ln_b"]),
        "sdown_bc": bcast(inputs["L0_ffn_s_down"]),
        "sg_in": scol(inputs["L0_ffn_s_gate"]),
        "su_in": scol(inputs["L0_ffn_s_up"]),
        "wgate": np.ascontiguousarray(
            np.asarray(inputs["L0_ffn_w_gate"]).reshape(C, 16, 512)
            .transpose(1, 0, 2)),
        "wup": np.ascontiguousarray(
            np.asarray(inputs["L0_ffn_w_up"]).reshape(C, 16, 512)
            .transpose(1, 0, 2)),
        "wdown": np.ascontiguousarray(np.asarray(inputs["L0_ffn_w_down"])),
    }
    in_maps = []
    for core in range(8):
        seq, half = core // 2, core % 2
        xo = np.asarray(x[seq, half * TOWN:(half + 1) * TOWN, :], f32)
        xext = np.zeros((TPAD, C), f32)
        xext[128:] = xo
        if half == 1:
            xext[127] = np.asarray(x[seq, TOWN - 1, :], f32)
        d = dict(common)
        d["x_ext"] = xext
        d["cmask"] = np.full((128, 1), 1.0 if half == 0 else 0.0, f32)
        d["imask"] = np.full((128, 1), 1.0 if half == 1 else 0.0, f32)
        in_maps.append(d)
    return in_maps


def kernel(**inputs):
    nc = _get_nc()
    in_maps = _prep_inputs(inputs)
    res = run_bass_kernel_spmd(nc, in_maps, core_ids=list(range(8)))
    full = np.empty((4, 4096, C), np.float32)
    for core in range(8):
        seq, half = core // 2, core % 2
        full[seq, half * TOWN:(half + 1) * TOWN, :] = res.results[core]["out"]
    return full


# version 07
# speedup vs baseline: 1.0329x; 1.0329x over previous
import contextlib
import numpy as np
import ml_dtypes
import concourse.bass as bass
import concourse.mybir as mybir
from concourse.bass_utils import run_bass_kernel_spmd

F32 = mybir.dt.float32
BF16 = mybir.dt.bfloat16
I8 = mybir.dt.int8
U8 = mybir.dt.uint8
AF = mybir.ActivationFunctionType
ALU = mybir.AluOpType
AX = mybir.AxisListType.X

C = 2048
TOWN = 2048
TPAD = 2176
NCC = 16
NTC = 16
NTCP = 17
L = 128
NCH = 16
H, HN = 32, 64
DI = 8192
NOC = 64
EPS_LN = 1e-5
EPS_GN = 1e-5 * 64.0

ENG = ("sync", "gpsimd", "tensor", "vector", "scalar")
USE_CC = True
P4_NCH = 99
P6_PARTS = frozenset(['corr','mm','yf','gn','yg','tr','wo'])
SINIT_ZERO = False
P4_PARTS = frozenset(['dec','elem','tr','D','s0','heads','supd'])


class Plan:

    def __init__(self, nc):
        self.nc = nc
        self.cnt = {}
        self.q = {e: [] for e in ENG}

    def val(self, s):
        return self.cnt.get(s, 0)

    def plan(self, eng, build=None, waits=(), inc=None):
        waits = tuple((s, v) for (s, v) in waits if v > 0)
        for s, v in waits:
            assert v <= self.val(s), f"wait on future value {v} > {self.val(s)}"
        if inc is not None:
            s, n = inc
            self.cnt[s] = self.val(s) + n
        self.q[eng].append((waits, build, inc))
        return self.val(inc[0]) if inc is not None else None

    def vdrain(self):
        self.plan("vector", lambda v: v.drain())

    def sdrain(self):
        self.plan("scalar", lambda sc: sc.drain())

    def dma(self, eng, out, in_, sem, waits=()):
        return self.plan(eng, (lambda e, o=out, i=in_: e.dma_start(out=o, in_=i)),
                         waits=waits, inc=(sem, 16))

    def barrier(self, sem):
        for e in ENG:
            self.plan(e, None, inc=(sem, 1))
        tot = self.val(sem)
        for e in ENG:
            self.plan(e, None, waits=[(sem, tot)])

    def flush(self, block):
        for e in ENG:
            ops = self.q[e]
            if not ops:
                continue
            self.q[e] = []

            def runner(engine, ops=ops):
                for waits, build, inc in ops:
                    for s, v in waits:
                        engine.wait_ge(s, v)
                    if build is not None:
                        ins = build(engine)
                        if inc is not None:
                            ins.then_inc(*inc)
                    elif inc is not None:
                        engine.sem_inc(inc[0], inc[1])

            getattr(block, e)(runner)


def r3(t, w):
    return t[:, :].rearrange("p (a t) -> p a t", t=w)


class Builder:
    def __init__(self, taps=(), stop_after=99):
        self.stop_after = stop_after
        nc = bass.Bass("TRN2", target_bir_lowering=False)
        self.nc = nc
        self.p = Plan(nc)
        taps = set(taps)
        self._taps = taps

        din = lambda n, sh, dt: nc.dram_tensor(
            n, list(sh), dt, kind="ExternalInput").ap()

        def scr(n, sh, dt):
            kind = "ExternalOutput" if n in taps else "Internal"
            return nc.dram_tensor(n, list(sh), dt, kind=kind).ap()

        self.x_ext = din("x_ext", (TPAD, C), F32)
        self.wq = {q: din(f"w_{q}", (C, C), F32) for q in "rkvg"}
        self.w_o = din("w_o", (C, C), F32)
        self.w1 = din("w1", (C, 160), F32)
        self.w2r = din("w2r", (32, 5 * C), F32)
        self.td1 = din("td1", (C, 64), F32)
        self.td2 = din("td2", (64, C), F32)
        self.maa = din("maa", (128, 16 * 7), F32)
        self.u_in = din("u_in", (128, 16), F32)
        self.lnw_in = din("lnw_bc", (128, C), F32)
        self.lnb_in = din("lnb_bc", (128, C), F32)
        self.sdown_in = din("sdown_bc", (128, C), F32)
        self.sg_in = din("sg_in", (128, 64), F32)
        self.su_in = din("su_in", (128, 64), F32)
        self.wgate = din("wgate", (16, C, 512), I8)
        self.wup = din("wup", (16, C, 512), I8)
        self.wdown = din("wdown", (DI, C), I8)
        self.cmask = din("cmask", (128, 1), F32)
        self.imask = din("imask", (128, 1), F32)

        self.out = nc.dram_tensor("out", [TOWN, C], F32, kind="ExternalOutput").ap()

        self.sp_xq = {q: scr(f"sp_xq_{q}", (NCC, 128, TOWN), BF16) for q in "wkvrg"}
        self.sp_xnb = scr("sp_xnb", (NCC, 128, TOWN), BF16)
        self.sp_e = scr("sp_e", (NCC, 128, TOWN), F32)
        self.sp_r = scr("sp_r", (NCC, 128, TOWN), BF16)
        self.sp_k = scr("sp_k", (NCC, 128, TOWN), BF16)
        self.sp_v = scr("sp_v", (NTC, 128, C), BF16)
        self.sp_g = scr("sp_g", (NTC, 128, C), BF16)
        self.sp_y = scr("sp_y", (NTC, 128, C), F32)
        self.sp_rp = scr("sp_rp", (NCH, 128, C), BF16)
        self.sp_part = scr("sp_part", (NTC, 128, C), F32)
        self.sp_fuse = scr("sp_fuse", (NOC, 128, TOWN), BF16)
        self.sp_sown = scr("sp_sown", (128, 1024), F32)
        self.sp_stat = scr("sp_stat", (NTCP, 128, 32), F32)
        self.sp_sall = scr("sp_sall", (NCH, 128, 1024), F32)
        self.sp_dec = scr("sp_dec", (NCH, 128, 64), F32)
        self.sp_ktm = scr("sp_ktm", (NCH, 128, C), BF16)
        self.sp_praw = scr("sp_praw", (NCH, 128, 1024), F32)
        self.sp_alx = scr("sp_alx", (NCH, 128, 1024), F32)
        self.sp_st2 = scr("sp_st2", (NCH, 128, 1024), F32)
        self.sp_stm = scr("sp_stm", (NCH, 128, 1024), F32)
        self.sp_ktf = scr("sp_ktf", (NCH, 128, C), BF16)
        self.cc_in = scr("cc_in", (128, 1024), F32)
        self.cc_out = scr("cc_out", (128, 1024), F32)

        self.c_idbf = nc.inline_tensor(
            np.eye(128, dtype=ml_dtypes.bfloat16), "c_idbf").ap()
        self.c_idf32 = nc.inline_tensor(
            np.eye(128, dtype=np.float32), "c_idf32").ap()
        self.c_masksu = nc.inline_tensor(
            np.triu(np.ones((128, 128), dtype=np.uint8), 1), "c_masksu").ap()
        mrow = np.zeros((128, 2), dtype=np.float32)
        mrow[0:64, 0] = 1.0
        mrow[64:128, 1] = 1.0
        self.c_mrow = nc.inline_tensor(mrow, "c_mrow").ap()

    def sb(self, name, shape, dt, stack):
        return stack.enter_context(self.nc.sbuf_tensor(name, list(shape), dt))

    def ps(self, name, shape, dt, stack):
        return stack.enter_context(self.nc.psum_tensor(name, list(shape), dt))

    def build(self):
        nc, p = self.nc, self.p
        gstack = contextlib.ExitStack()
        with (
            nc.semaphore("sD") as sD,
            nc.semaphore("sG") as sG,
            nc.semaphore("sT") as sT,
            nc.semaphore("sV") as sV,
            nc.semaphore("sS") as sS,
            nc.semaphore("sC") as sC,
            nc.semaphore("sB") as sB,
            nc.Block() as block,
        ):
            self.sD, self.sG, self.sT, self.sV, self.sS = sD, sG, sT, sV, sS
            self.sC, self.sB = sC, sB

            g = gstack
            self.idbf = self.sb("idbf", (128, 128), BF16, g)
            self.idf32 = self.sb("idf32", (128, 128), F32, g)
            self.masksu = self.sb("masksu", (128, 128), U8, g)
            self.maa_sb = self.sb("maa_sb", (128, 16 * 7), F32, g)
            self.u_sb = self.sb("u_sb", (128, 16), F32, g)
            self.cmask_sb = self.sb("cmask_sb", (128, 1), F32, g)
            self.imask_sb = self.sb("imask_sb", (128, 1), F32, g)
            self.ones = self.sb("ones", (128, 128), F32, g)
            self.mrow = self.sb("mrow", (128, 2), F32, g)

            self.xnT_stack = contextlib.ExitStack()
            self.xnT = self.sb("xnT", (128, NCC * TPAD), F32, self.xnT_stack)

            self.p0_consts()
            if self.stop_after >= 1:
                self.p1_ln()
                p.barrier(sB)
            if self.stop_after >= 2:
                self.p2_mix()
                p.barrier(sB)
            self.xnT_stack.close()
            if self.stop_after >= 3:
                self.p3_proj()
                p.barrier(sB)
            if self.stop_after >= 4:
                self.p4_scan()
                p.barrier(sB)
            if self.stop_after >= 5:
                self.p5_ffn1()
                p.barrier(sB)
            if self.stop_after >= 6:
                self.p6_gn()
                p.barrier(sB)
            if self.stop_after >= 7:
                self.p7_ffn2()
            p.flush(block)
            gstack.close()
        return nc

    def maa_col(self, j, cc):
        return self.maa_sb[:, cc * 7 + j: cc * 7 + j + 1]

    def p0_consts(self):
        p, sD, sV = self.p, self.sD, self.sV
        p.dma("sync", self.idbf[:, :], self.c_idbf, sD)
        p.dma("sync", self.idf32[:, :], self.c_idf32, sD)
        p.dma("sync", self.masksu[:, :], self.c_masksu, sD)
        p.dma("sync", self.maa_sb[:, :], self.maa, sD)
        p.dma("sync", self.u_sb[:, :], self.u_in, sD)
        p.dma("sync", self.cmask_sb[:, :], self.cmask, sD)
        p.dma("sync", self.mrow[:, :], self.c_mrow, sD)
        p.dma("sync", self.imask_sb[:, :], self.imask, sD)
        d = p.val(sD)
        p.plan("vector", lambda v: v.memset(self.ones[:, :], 1.0), inc=(sV, 1))
        vv = p.val(sV)
        for e in ENG:
            p.plan(e, None, waits=[(sD, d), (sV, vv)])

    def p1_ln(self):
        nc, p = self.nc, self.p
        sD, sT, sV, sS = self.sD, self.sT, self.sV, self.sS
        es = contextlib.ExitStack()
        xt = [self.sb(f"xt{i}", (128, C), F32, es) for i in range(2)]
        stat = self.sb("stat", (128, 32), F32, es)
        dump = self.sb("dump", (128, C), F32, es)
        pst = [self.ps(f"p1t{i}", (128, 512), F32, es) for i in range(4)]
        xnT3 = r3(self.xnT, TPAD)

        tr_tok = [0, 0]
        evac_tok = [None, None, None, None]
        for tc in range(NTCP):
            bi = tc % 2
            buf = xt[bi]
            so = bi * 16
            dv = p.dma("sync", buf[:, :], self.x_ext[tc * 128:(tc + 1) * 128, :],
                       sD, waits=[(sT, tr_tok[bi])])
            p.plan("vector", lambda v, b=buf, so=so: v.reduce_sum(
                stat[:, so:so + 1], b[:, :], axis=AX), waits=[(sD, dv)])
            p.plan("vector", lambda v, b=buf: v.tensor_mul(
                dump[:, :], b[:, :], b[:, :]))
            p.plan("vector", lambda v, so=so: v.reduce_sum(
                stat[:, so + 1:so + 2], dump[:, :], axis=AX))
            p.vdrain()
            p.plan("vector", lambda v, so=so: v.tensor_scalar(
                stat[:, so + 2:so + 3], stat[:, so:so + 1], 1.0 / C, None,
                ALU.mult))
            p.plan("vector", lambda v, so=so: v.tensor_scalar(
                stat[:, so + 3:so + 4], stat[:, so + 1:so + 2], 1.0 / C, None,
                ALU.mult))
            p.vdrain()
            p.plan("vector", lambda v, so=so: v.tensor_mul(
                stat[:, so + 4:so + 5], stat[:, so + 2:so + 3],
                stat[:, so + 2:so + 3]))
            p.vdrain()
            p.plan("vector", lambda v, so=so: v.tensor_sub(
                stat[:, so + 5:so + 6], stat[:, so + 3:so + 4],
                stat[:, so + 4:so + 5]))
            p.vdrain()
            p.plan("vector", lambda v, so=so: v.tensor_scalar(
                stat[:, so + 6:so + 7], stat[:, so + 5:so + 6], EPS_LN, None,
                ALU.add), inc=(sV, 1))
            vv = p.val(sV)
            p.plan("scalar", lambda s, so=so: s.activation(
                stat[:, so + 7:so + 8], stat[:, so + 6:so + 7], AF.Sqrt),
                waits=[(sV, vv)], inc=(sS, 1))
            sv = p.val(sS)
            p.plan("vector", lambda v, so=so: v.reciprocal(
                stat[:, so + 8:so + 9], stat[:, so + 7:so + 8]),
                waits=[(sS, sv)])
            p.vdrain()
            p.plan("vector", lambda v, so=so: v.tensor_mul(
                stat[:, so + 9:so + 10], stat[:, so + 2:so + 3],
                stat[:, so + 8:so + 9]))
            p.vdrain()
            p.plan("vector", lambda v, so=so: v.tensor_scalar(
                stat[:, so + 10:so + 11], stat[:, so + 9:so + 10], -1.0, None,
                ALU.mult))
            p.vdrain()
            p.plan("vector", lambda v, b=buf, so=so: v.tensor_scalar(
                b[:, :], b[:, :], stat[:, so + 8:so + 9],
                stat[:, so + 10:so + 11], ALU.mult, ALU.add), inc=(sV, 1))
            ntok = p.val(sV)
            for bk in range(4):
                for q4 in range(4):
                    cc = bk * 4 + q4
                    w = [(sV, ntok)]
                    if q4 == 0 and evac_tok[bk] is not None:
                        w.append(evac_tok[bk])
                    p.plan("tensor", lambda t, b=buf, cc=cc, bk=bk, q4=q4:
                           t.transpose(pst[bk][:, q4 * 128:(q4 + 1) * 128],
                                       b[:, cc * 128:(cc + 1) * 128],
                                       self.idf32[:, :]),
                           waits=w, inc=((sT, 1) if q4 == 3 else None))
                tb = p.val(sT)
                if bk % 2 == 0:
                    p.plan("vector", lambda v, bk=bk, tc=tc: v.tensor_copy(
                        xnT3[:, bk * 4:(bk + 1) * 4, tc * 128:(tc + 1) * 128],
                        pst[bk][:, :]), waits=[(sT, tb)], inc=(sV, 1))
                    evac_tok[bk] = (sV, p.val(sV))
                else:
                    p.plan("scalar", lambda s, bk=bk, tc=tc: s.activation(
                        xnT3[:, bk * 4:(bk + 1) * 4, tc * 128:(tc + 1) * 128],
                        pst[bk][:, :], AF.Copy), waits=[(sT, tb)], inc=(sS, 1))
                    evac_tok[bk] = (sS, p.val(sS))
            tr_tok[bi] = p.val(sT)
            if "sp_stat" in getattr(self, "_taps", ()):
                p.dma("sync", self.sp_stat[tc, :, :], stat[:, :], sD,
                      waits=[(sV, ntok)])
        es.close()

    def p2_mix(self):
        nc, p = self.nc, self.p
        sD, sG, sT, sV, sS = self.sD, self.sG, self.sT, self.sV, self.sS
        es = contextlib.ExitStack()
        w1_sb = self.sb("w1_sb", (128, 16 * 160), BF16, es)
        w2_sb = self.sb("w2_sb", (32, 5 * C), BF16, es)
        m_sb = self.sb("m_sb", (32, 5 * C), BF16, es)
        xxs = self.sb("xxs", (128, C), F32, es)
        xxxb = self.sb("xxxb", (128, 512), BF16, es)
        xqa = [self.sb(f"xqa{i}", (128, C), BF16, es) for i in range(2)]
        t1b = self.sb("t1b", (128, 512), F32, es)
        xnb = self.sb("xnb", (128, C), BF16, es)
        ps_m = [self.ps(f"p2q{i}", (32, 512), F32, es) for i in range(5)]
        ps_mix = [self.ps(f"p2m{i}", (128, 512), F32, es) for i in range(2)]
        xnT3 = r3(self.xnT, TPAD)
        w1_3 = r3(w1_sb, 160)
        xxs3 = r3(xxs, 512)

        p.dma("gpsimd", w1_3[:, :, :],
              self.w1.rearrange("(a p) m -> p a m", p=128), sG)
        gd = p.dma("gpsimd", w2_sb[:, :], self.w2r, sG)

        m_ev = 0
        xxx_tok = 0
        for nch in range(4):
            t0 = 128 + nch * 512
            for cc in range(NCC):
                p.plan("vector", lambda v, cc=cc, t0=t0: v.tensor_sub(
                    xxs3[:, 0, :], xnT3[:, cc, t0 - 1:t0 + 511],
                    xnT3[:, cc, t0:t0 + 512]),
                    waits=([(sT, xxx_tok)] if xxx_tok else []))
                p.vdrain()
                p.plan("vector", lambda v, cc=cc, t0=t0: v.scalar_tensor_tensor(
                    xxxb[:, :], xxs3[:, 0, :], self.maa_col(0, cc),
                    xnT3[:, cc, t0:t0 + 512], ALU.mult, ALU.add),
                    inc=(sV, 1))
                xv = p.val(sV)
                for q5 in range(5):
                    w = [(sV, xv), (sG, gd)]
                    if cc == 0 and m_ev:
                        w.append((sS, m_ev))
                    p.plan("tensor", lambda t, cc=cc, q5=q5: t.matmul(
                        ps_m[q5][:, :], w1_3[:, cc, 32 * q5:32 * q5 + 32],
                        xxxb[:, :], start=(cc == 0), stop=(cc == NCC - 1)),
                        waits=(w if (cc == 0 or q5 == 0) else ()),
                        inc=((sT, 1) if q5 == 4 else None))
                xxx_tok = p.val(sT)
            tt = p.val(sT)
            for q5 in range(5):
                p.plan("scalar", lambda s, q5=q5, nch=nch: s.activation(
                    m_sb[0:32, q5 * C + nch * 512: q5 * C + (nch + 1) * 512],
                    ps_m[q5][:, :], AF.Tanh),
                    waits=([(sT, tt)] if q5 == 0 else []),
                    inc=((sS, 1) if q5 == 4 else None))
            m_ev = p.val(sS)
        m_done = p.val(sS)

        QL = "wkvrg"
        mix_ev = [None, None]
        xx_done = {}
        spill_tok = [0, 0]
        for cc in range(NCC):
            t0 = 128
            p.plan("vector", lambda v, cc=cc: v.tensor_sub(
                xxs[:, :], xnT3[:, cc, 127:127 + TOWN],
                xnT3[:, cc, 128:128 + TOWN]), inc=(sV, 1))
            p.vdrain()
            xx_v = p.val(sV)
            for qi, q in enumerate(QL):
                ab = (cc * 5 + qi) % 2
                for nch in range(4):
                    bk = (qi * 4 + nch) % 2
                    w = [(sS, m_done), (sG, gd)]
                    if mix_ev[bk] is not None:
                        w.append(mix_ev[bk])
                    p.plan("tensor", lambda t, cc=cc, qi=qi, nch=nch, bk=bk:
                           t.matmul(ps_mix[bk][:, :],
                                    w2_sb[0:32, qi * C + cc * 128:
                                          qi * C + cc * 128 + 128],
                                    m_sb[0:32, qi * C + nch * 512:
                                         qi * C + (nch + 1) * 512],
                                    start=True, stop=True),
                           waits=w, inc=(sT, 1))
                    tt = p.val(sT)
                    p.plan("vector", lambda v, bk=bk, cc=cc, qi=qi, nch=nch:
                           v.scalar_tensor_tensor(
                               t1b[:, :], ps_mix[bk][:, :],
                               self.maa_col(1 + qi, cc),
                               xxs[:, nch * 512:(nch + 1) * 512],
                               ALU.add, ALU.mult),
                           waits=[(sT, tt), (sV, xx_v)] + (
                               [(sD, spill_tok[ab])]
                               if (nch == 0 and spill_tok[ab]) else []))
                    p.vdrain()
                    p.plan("vector", lambda v, ab=ab, cc=cc, nch=nch:
                           v.tensor_add(
                               xqa[ab][:, nch * 512:(nch + 1) * 512],
                               t1b[:, :],
                               xnT3[:, cc, 128 + nch * 512:
                                    128 + (nch + 1) * 512]),
                           inc=(sV, 1))
                    mix_ev[bk] = (sV, p.val(sV))
                vv = p.val(sV)
                p.dma("sync", self.sp_xq[q][cc, :, :], xqa[ab][:, :], sD,
                      waits=[(sV, vv)])
                spill_tok[ab] = p.val(sD)
        xnb_d = 0
        for cc in range(NCC):
            w = [(sD, xnb_d)] if xnb_d else []
            p.plan("scalar", lambda s, cc=cc: s.activation(
                xnb[:, :], xnT3[:, cc, 128:128 + TOWN], AF.Copy),
                waits=w, inc=(sS, 1))
            sv = p.val(sS)
            p.dma("sync", self.sp_xnb[cc, :, :], xnb[:, :], sD,
                  waits=[(sS, sv)])
            xnb_d = p.val(sD)
        es.close()

    def p3_proj(self):
        nc, p = self.nc, self.p
        sD, sG, sT, sV, sS = self.sD, self.sG, self.sT, self.sV, self.sS

        es = contextlib.ExitStack()
        xq_sb = self.sb("xq_sb", (128, NCC * TOWN), BF16, es)
        td1_sb = self.sb("td1_sb", (128, 16 * 64), BF16, es)
        td2_sb = self.sb("td2_sb", (64, C), BF16, es)
        tdt = self.sb("tdt", (64, TOWN), BF16, es)
        ebuf = [self.sb(f"ebuf{i}", (128, TOWN), F32, es) for i in range(2)]
        ps_d = self.ps("p3d", (64, 512), F32, es)
        ps_e = [self.ps(f"p3e{i}", (128, 512), F32, es) for i in range(2)]
        xq3 = r3(xq_sb, TOWN)
        td1_3 = r3(td1_sb, 64)

        p.dma("gpsimd", td1_3[:, :, :],
              self.td1.rearrange("(a p) d -> p a d", p=128), sG)
        gd1 = p.dma("gpsimd", td2_sb[:, :], self.td2, sG)
        ld = p.dma("sync", xq3[:, :, :],
                   self.sp_xq["w"].rearrange("a p t -> p a t"), sD)
        tanh_tok = 0
        for nch in range(4):
            for cc in range(NCC):
                w = [(sG, gd1), (sD, ld)]
                if cc == 0 and nch > 0:
                    w.append((sS, tanh_tok))
                p.plan("tensor", lambda t, cc=cc, nch=nch: t.matmul(
                    ps_d[:, :], td1_3[:, cc, :],
                    xq3[:, cc, nch * 512:(nch + 1) * 512],
                    start=(cc == 0), stop=(cc == NCC - 1)),
                    waits=(w if cc == 0 else ()),
                    inc=((sT, 1) if cc == NCC - 1 else None))
            tt = p.val(sT)
            p.plan("scalar", lambda s, nch=nch: s.activation(
                tdt[:, nch * 512:(nch + 1) * 512], ps_d[:, :], AF.Tanh),
                waits=[(sT, tt)], inc=(sS, 1))
            tanh_tok = p.val(sS)
        e_ev = [0, 0]
        e_dma = [0, 0]
        for cc in range(NCC):
            bi = cc % 2
            for nch in range(4):
                bk = (cc * 4 + nch) % 2
                w = [(sS, tanh_tok)]
                if e_ev[bk]:
                    w.append((sS, e_ev[bk]))
                p.plan("tensor", lambda t, cc=cc, nch=nch, bk=bk: t.matmul(
                    ps_e[bk][:, :], td2_sb[:, cc * 128:(cc + 1) * 128],
                    tdt[:, nch * 512:(nch + 1) * 512], start=True, stop=True),
                    waits=w, inc=(sT, 1))
                tt = p.val(sT)
                w2_ = [(sT, tt)]
                if nch == 0 and e_dma[bi]:
                    w2_.append((sD, e_dma[bi]))
                p.plan("scalar", lambda s, cc=cc, nch=nch, bk=bk, bi=bi:
                       s.activation(ebuf[bi][:, nch * 512:(nch + 1) * 512],
                                    ps_e[bk][:, :], AF.Exp,
                                    bias=self.maa_col(6, cc)),
                       waits=w2_, inc=(sS, 1))
                e_ev[bk] = p.val(sS)
            sv = p.val(sS)
            p.dma("sync", self.sp_e[cc, :, :], ebuf[bi][:, :], sD,
                  waits=[(sS, sv)])
            e_dma[bi] = p.val(sD)
        es.close()
        p.barrier(self.sB)

        for q, fm in (("k", True), ("r", True), ("v", False), ("g", False)):
            es = contextlib.ExitStack()
            xq_sb = self.sb(f"xq_sb_{q}", (128, NCC * TOWN), BF16, es)
            w_sb = self.sb(f"w_sb_{q}", (128, NCC * C), BF16, es)
            ob = [self.sb(f"ob_{q}{i}", (128, C), BF16, es) for i in range(2)]
            psm = [self.ps(f"p3m_{q}{i}", (128, 512), F32, es) for i in range(2)]
            xq3 = r3(xq_sb, TOWN)
            w3 = r3(w_sb, C)
            ld = p.dma("sync", xq3[:, :, :],
                       self.sp_xq[q].rearrange("a p t -> p a t"), sD)
            gd = p.dma("gpsimd", w3[:, :, :],
                       self.wq[q].rearrange("(a p) j -> p a j", p=128), sG)
            ev_tok = [None, None]
            dma_tok = [0, 0]
            spill = {"r": self.sp_r, "k": self.sp_k,
                     "v": self.sp_v, "g": self.sp_g}[q]
            for oc in range(16):
                bi = oc % 2
                for nch in range(4):
                    bk = (oc * 4 + nch) % 2
                    w = [(sD, ld), (sG, gd)]
                    if ev_tok[bk] is not None:
                        w.append(ev_tok[bk])
                    for cc in range(NCC):
                        def mm(t, cc=cc, oc=oc, nch=nch, bk=bk, fm=fm):
                            if fm:
                                lhsT = w3[:, cc, oc * 128:(oc + 1) * 128]
                                rhs = xq3[:, cc, nch * 512:(nch + 1) * 512]
                            else:
                                lhsT = xq3[:, cc, oc * 128:(oc + 1) * 128]
                                rhs = w3[:, cc, nch * 512:(nch + 1) * 512]
                            return t.matmul(psm[bk][:, :], lhsT, rhs,
                                            start=(cc == 0),
                                            stop=(cc == NCC - 1))
                        p.plan("tensor", mm, waits=(w if cc == 0 else ()),
                               inc=((sT, 1) if cc == NCC - 1 else None))
                    tt = p.val(sT)
                    wdma = [(sD, dma_tok[bi])] if (nch == 0 and dma_tok[bi]) \
                        else []
                    if q == "g":
                        p.plan("scalar", lambda s, bi=bi, nch=nch, bk=bk:
                               s.activation(
                                   ob[bi][:, nch * 512:(nch + 1) * 512],
                                   psm[bk][:, :], AF.Silu),
                               waits=[(sT, tt)] + wdma, inc=(sS, 1))
                        ev_tok[bk] = (sS, p.val(sS))
                    else:
                        p.plan("vector", lambda v, bi=bi, nch=nch, bk=bk:
                               v.tensor_copy(
                                   ob[bi][:, nch * 512:(nch + 1) * 512],
                                   psm[bk][:, :]),
                               waits=[(sT, tt)] + wdma, inc=(sV, 1))
                        ev_tok[bk] = (sV, p.val(sV))
                sem = sS if q == "g" else sV
                sv = p.val(sem)
                p.dma("sync", spill[oc, :, :], ob[bi][:, :], sD,
                      waits=[(sem, sv)])
                dma_tok[bi] = p.val(sD)
            es.close()
            p.barrier(self.sB)

    def p4_scan(self):
        nc, p = self.nc, self.p
        sD, sG, sT, sV, sS = self.sD, self.sG, self.sT, self.sV, self.sS
        sC = self.sC
        es = contextlib.ExitStack()
        r_ch = [self.sb(f"r_ch{i}", (128, C), BF16, es) for i in range(2)]
        k_ch = [self.sb(f"k_ch{i}", (128, C), BF16, es) for i in range(2)]
        v_ch = [self.sb(f"v_ch{i}", (128, C), BF16, es) for i in range(2)]
        e_ch = [self.sb(f"e_ch{i}", (128, C), F32, es) for i in range(2)]
        cs = self.sb("cs", (128, 16 * 129), F32, es)
        csc = self.sb("csc", (128, 16 * 129), F32, es)
        Cc = self.sb("Cc", (128, 16), F32, es)
        AL = self.sb("AL", (128, 16), F32, es)
        sdel = self.sb("sdel", (128, 16), F32, es)
        expmC = self.sb("expmC", (128, 16), F32, es)
        rpsc = self.sb("rpsc", (128, 16), F32, es)
        Dpref = self.sb("Dpref", (128, 16), F32, es)
        dec_r = self.sb("dec_r", (128, C), BF16, es)
        dec_k = self.sb("dec_k", (128, C), BF16, es)
        rt = self.sb("rt", (128, C), BF16, es)
        kt = self.sb("kt", (128, C), BF16, es)
        rk = self.sb("rk", (128, C), BF16, es)
        ruk = self.sb("ruk", (128, C), BF16, es)
        kt_tm = self.sb("kt_tm", (128, C), BF16, es)
        ruk_tm = self.sb("ruk_tm", (128, C), BF16, es)
        Dv = self.sb("Dv", (128, 32), F32, es)
        S0b = self.sb("S0b", (128, 1024), BF16, es)
        S0b_ev = self.sb("S0b_ev", (128, 1024), BF16, es)
        S0b_od = self.sb("S0b_od", (128, 1024), BF16, es)
        ptm = [self.sb(f"ptm{i}", (128, 128), BF16, es) for i in range(2)]
        dvb = [self.sb(f"dvb{i}", (128, 512), F32, es) for i in range(2)]
        stmp = self.sb("stmp", (128, 1024), F32, es)
        sdel_x = self.sb("sdel_x", (128, 1024), F32, es)
        al_x = self.sb("al_x", (128, 1024), F32, es)
        emc_x = self.sb("emc_x", (128, 1024), F32, es)
        s_t2 = self.sb("s_t2", (128, 1024), F32, es)
        ytile = [self.sb(f"yt{i}", (128, C), F32, es) for i in range(2)]
        rpb = [self.sb(f"rp{i}", (128, C), BF16, es) for i in range(2)]
        ps_t1 = self.ps("p4t1", (128, 1024), BF16, es)
        ps_t2 = self.ps("p4t2", (128, 1024), BF16, es)
        ps_pt = [self.ps(f"p4pt{i}", (128, 512), F32, es) for i in range(2)]
        ps_y = [self.ps(f"p4y{i}", (128, 512), F32, es) for i in range(2)]
        ps_s = self.ps("p4s", (128, 1024), F32, es)
        self.S_a = self.sb("S_a", (128, 1024), F32, es)
        self.S_b = self.sb("S_b", (128, 1024), F32, es)
        cs3 = r3(cs, 129)
        csc3 = r3(csc, 129)

        p.plan("vector", lambda v: v.memset(cs[:, :], 0.0))
        p.plan("vector", lambda v: v.memset(self.S_a[:, :], 0.0))
        p.plan("vector", lambda v: v.memset(dvb[0][:, :], 0.0))
        p.plan("vector", lambda v: v.memset(dvb[1][:, :], 0.0))
        p.plan("vector", lambda v: v.memset(ptm[0][:, :], 0.0))
        p.plan("vector", lambda v: v.memset(ptm[1][:, :], 0.0))
        p.plan("vector", lambda v: v.memset(Dpref[:, :], 1.0), inc=(sV, 1))
        init_v = p.val(sV)

        ld_tok = [0, 0]
        endT = [0, 0]
        ydma = [0, 0]
        dec_tok = 0
        prevT = 0
        tr_ev = [None, None]
        tr2_ev = [None, None]
        pt_ev = [None, None]
        ptfree = [0, 0]
        y_ev = [None, None]
        s_ev = init_v

        def chunk_load(ch):
            bi = ch % 2
            w = []
            if ch >= 2:
                w = [(sT, endT[bi]), (sV, p.val(sV))]
            p.dma("sync", r3(r_ch[bi], 128)[:, :, :],
                  self.sp_r.rearrange("a p t -> p a t")
                  [:, :, ch * 128:(ch + 1) * 128], sD, waits=w)
            p.dma("sync", r3(k_ch[bi], 128)[:, :, :],
                  self.sp_k.rearrange("a p t -> p a t")
                  [:, :, ch * 128:(ch + 1) * 128], sD)
            p.dma("sync", r3(e_ch[bi], 128)[:, :, :],
                  self.sp_e.rearrange("a p t -> p a t")
                  [:, :, ch * 128:(ch + 1) * 128], sD)
            p.dma("sync", v_ch[bi][:, :], self.sp_v[ch, :, :], sD)
            ld_tok[bi] = p.val(sD)

        chunk_load(0)
        chunk_load(1)
        for ch in range(min(NCH, P4_NCH)):
            bi = ch % 2
            S_cur = self.S_a if ch % 2 == 0 else self.S_b
            S_nxt = self.S_b if ch % 2 == 0 else self.S_a
            for cc in range(NCC):
                w = []
                if cc == 0:
                    w = [(sD, ld_tok[bi])]
                    if dec_tok:
                        w.append((sS, dec_tok))
                p.plan("vector", lambda v, cc=cc, bi=bi: v.tensor_tensor_scan(
                    cs3[:, cc, 1:129], self.ones[:, :],
                    e_ch[bi][:, cc * 128:(cc + 1) * 128], 0.0,
                    ALU.mult, ALU.add), waits=w)
            p.vdrain()
            p.plan("vector", lambda v: v.tensor_scalar(
                Cc[:, :], cs3[:, :, 128:129], 0.5, None, ALU.mult))
            p.vdrain()
            for cc in range(NCC):
                p.plan("vector", lambda v, cc=cc: v.tensor_scalar(
                    csc3[:, cc, 0:129], cs3[:, cc, 0:129],
                    Cc[:, cc:cc + 1], None, ALU.subtract), inc=(sV, 1))
            cs_v = p.val(sV)
            if 'dec' not in P4_PARTS:
                continue
            w = [(sV, cs_v), (sV, s_ev)]
            p.plan("scalar", lambda s: s.activation(
                dec_r[:, :], csc3[:, :, 0:128], AF.Exp, scale=-1.0),
                waits=[(sV, max(cs_v, s_ev))])
            p.plan("scalar", lambda s: s.activation(
                dec_k[:, :], csc3[:, :, 1:129], AF.Exp))
            p.plan("scalar", lambda s: s.activation(
                AL[:, :], cs3[:, :, 128:129], AF.Exp, scale=-1.0))
            p.plan("scalar", lambda s: s.activation(
                sdel[:, :], csc3[:, :, 128:129], AF.Exp, scale=-1.0))
            p.plan("scalar", lambda s: s.activation(
                expmC[:, :], Cc[:, :], AF.Exp, scale=-1.0), inc=(sS, 1))
            dec_tok = p.val(sS)
            if "sp_dec" in self._taps:
                p.dma("sync", self.sp_dec[ch, :, 0:16], sdel[:, :], sD,
                      waits=[(sS, dec_tok)])
                p.dma("sync", self.sp_dec[ch, :, 16:32], AL[:, :], sD)
                p.dma("sync", self.sp_dec[ch, :, 32:48], expmC[:, :], sD)
                p.dma("sync", self.sp_dec[ch, :, 48:64], Cc[:, :], sD)
            wexp = [(sS, dec_tok), (sT, prevT), (sV, s_ev)]
            for kk in range(16):
                p.plan("vector", lambda v, kk=kk: v.tensor_scalar(
                    sdel_x[:, kk * 64:(kk + 1) * 64], self.ones[:, 0:64],
                    sdel[:, kk:kk + 1], None, ALU.mult),
                    waits=([(a, b) for a, b in wexp if b] if kk == 0 else ()))
            for kk in range(16):
                p.plan("vector", lambda v, kk=kk: v.tensor_scalar(
                    al_x[:, kk * 64:(kk + 1) * 64], self.ones[:, 0:64],
                    AL[:, kk:kk + 1], None, ALU.mult))
            for kk in range(16):
                p.plan("vector", lambda v, kk=kk: v.tensor_scalar(
                    emc_x[:, kk * 64:(kk + 1) * 64], self.ones[:, 0:64],
                    expmC[:, kk:kk + 1], None, ALU.mult))
            p.vdrain()
            p.plan("vector", None, inc=(sV, 1))
            exp_v = p.val(sV)
            if 'elem' not in P4_PARTS:
                continue
            w = [(sS, dec_tok), (sT, prevT)]
            p.plan("vector", lambda v, bi=bi: v.tensor_mul(
                rt[:, :], r_ch[bi][:, :], dec_r[:, :]), waits=w)
            p.plan("vector", lambda v, bi=bi: v.tensor_mul(
                kt[:, :], k_ch[bi][:, :], dec_k[:, :]))
            p.plan("vector", lambda v, bi=bi: v.tensor_mul(
                rk[:, :], r_ch[bi][:, :], k_ch[bi][:, :]))
            p.vdrain()
            for cc in range(NCC):
                p.plan("vector", lambda v, cc=cc: v.tensor_scalar(
                    ruk[:, cc * 128:(cc + 1) * 128],
                    rk[:, cc * 128:(cc + 1) * 128],
                    self.u_sb[:, cc:cc + 1], None, ALU.mult))
            p.plan("vector", lambda v: v.tensor_mul(
                rpsc[:, :], expmC[:, :], Dpref[:, :]))
            p.vdrain()
            rp_w = [(sD, ydma[bi])] if ydma[bi] else []
            for cc in range(NCC):
                p.plan("vector", lambda v, cc=cc, bi=bi: v.tensor_scalar(
                    rpb[bi][:, cc * 128:(cc + 1) * 128],
                    rt[:, cc * 128:(cc + 1) * 128],
                    rpsc[:, cc:cc + 1], None, ALU.mult),
                    waits=(rp_w if cc == 0 else ()))
            p.plan("vector", lambda v: v.tensor_mul(
                Dpref[:, :], Dpref[:, :], AL[:, :]), inc=(sV, 1))
            elem_v = p.val(sV)
            if 'tr' not in P4_PARTS:
                continue
            for half in range(2):
                for j in range(8):
                    cc = half * 8 + j
                    w = [(sV, elem_v)]
                    if j == 0 and tr_ev[0] is not None:
                        w.append(tr_ev[0])
                    p.plan("tensor", lambda t, cc=cc, j=j: t.transpose(
                        ps_t1[:, j * 128:(j + 1) * 128],
                        kt[:, cc * 128:(cc + 1) * 128], self.idbf[:, :]),
                        waits=w, inc=((sT, 1) if j == 7 else None))
                tt = p.val(sT)
                p.plan("vector", lambda v, half=half: v.tensor_copy(
                    kt_tm[:, half * 1024:(half + 1) * 1024], ps_t1[:, :]),
                    waits=[(sT, tt)], inc=(sV, 1))
                tr_ev[0] = (sV, p.val(sV))
            for half in range(2):
                for j in range(8):
                    cc = half * 8 + j
                    w = [(sV, elem_v)]
                    if j == 0 and tr2_ev[0] is not None:
                        w.append(tr2_ev[0])
                    p.plan("tensor", lambda t, cc=cc, j=j: t.transpose(
                        ps_t2[:, j * 128:(j + 1) * 128],
                        ruk[:, cc * 128:(cc + 1) * 128], self.idbf[:, :]),
                        waits=w, inc=((sT, 1) if j == 7 else None))
                tt = p.val(sT)
                p.plan("scalar", lambda s, half=half: s.activation(
                    ruk_tm[:, half * 1024:(half + 1) * 1024], ps_t2[:, :],
                    AF.Copy), waits=[(sT, tt)], inc=(sS, 1))
                tr2_ev[0] = (sS, p.val(sS))
            ktm_v = tr_ev[0][1]
            ruk_s = tr2_ev[0][1]
            if "sp_ktm" in self._taps:
                p.dma("sync", self.sp_ktm[ch, :, :], kt_tm[:, :], sD,
                      waits=[(sV, ktm_v)])
                p.dma("sync", self.sp_ktf[ch, :, :], kt[:, :], sD)
            if 'D' not in P4_PARTS:
                continue
            for h in range(H):
                p.plan("vector", lambda v, h=h: v.reduce_sum(
                    Dv[:, h:h + 1], ruk_tm[:, h * 64:(h + 1) * 64], axis=AX),
                    waits=([(sS, ruk_s)] if h == 0 else ()),
                    inc=((sV, 1) if h == H - 1 else None))
            d_tok = p.val(sV)
            if 's0' not in P4_PARTS:
                continue
            p.plan("vector", lambda v, S_cur=S_cur: v.tensor_mul(
                S0b[:, :], S_cur[:, :], emc_x[:, :]),
                waits=[(sV, exp_v)])
            p.vdrain()
            p.plan("vector", lambda v: v.tensor_scalar(
                S0b_ev[:, :], S0b[:, :], self.mrow[:, 0:1], None, ALU.mult))
            p.plan("vector", lambda v: v.tensor_scalar(
                S0b_od[:, :], S0b[:, :], self.mrow[:, 1:2], None, ALU.mult),
                inc=(sV, 1))
            s0_tok = ("V", p.val(sV))
            if 'heads' not in P4_PARTS:
                continue
            ygrp_tok = [0] * 4
            sdel_grp = [0, 0]
            for h in range(H):
                po = 64 * (h % 2)
                cc = h // 2
                hs = slice(po, po + 64)
                fs = slice(cc * 128, cc * 128 + 128)
                bk = h % 2
                grp = h // 8
                yb = grp % 2
                w = [(sV, elem_v)]
                if pt_ev[bk] is not None:
                    w.append(pt_ev[bk])
                p.plan("tensor", lambda t, hs=hs, fs=fs, bk=bk: t.matmul(
                    ps_pt[bk][:, 0:128], kt[hs, fs], rt[hs, fs],
                    start=True, stop=True), waits=w, inc=(sT, 1))
                tt = p.val(sT)
                w = [(sT, tt)]
                if ptfree[bk]:
                    w.append((sT, ptfree[bk]))
                p.plan("vector", lambda v, bk=bk: v.copy_predicated(
                    ptm[bk][:, :], self.masksu[:, :], ps_pt[bk][:, 0:128]),
                    waits=w, inc=(sV, 1))
                pt_ev[bk] = (sV, p.val(sV))
                w = [pt_ev[bk]]
                if h % 8 == 0:
                    if y_ev[yb] is not None:
                        w.append(y_ev[yb])
                    w.append((sV, s0_tok[1]))
                p.plan("tensor", lambda t, h=h, yb=yb, bk=bk, bi=bi: t.matmul(
                    ps_y[yb][:, (h % 8) * 64:(h % 8) * 64 + 64],
                    ptm[bk][:, :], v_ch[bi][:, h * 64:(h + 1) * 64],
                    start=(h % 8 == 0), stop=False), waits=w)
                p.plan("tensor", lambda t, h=h, yb=yb, fs=fs: t.matmul(
                    ps_y[yb][:, (h % 8) * 64:(h % 8) * 64 + 64],
                    rt[:, fs],
                    (S0b_ev if h % 2 == 0 else S0b_od)
                    [:, 64 * (h // 2):64 * (h // 2) + 64],
                    start=False, stop=True), inc=(sT, 1))
                ptfree[bk] = p.val(sT)
                if h % 8 == 7:
                    ygrp_tok[grp] = p.val(sT)
                w = [(sV, d_tok)]
                if h % 8 == 0 and y_ev[yb] is not None:
                    w.append(y_ev[yb])
                p.plan("vector", lambda v, h=h, yb=yb, bi=bi: v.tensor_scalar(
                    dvb[yb][:, (h % 8) * 64:(h % 8) * 64 + 64],
                    v_ch[bi][:, h * 64:(h + 1) * 64],
                    Dv[:, h:h + 1], None, ALU.mult), waits=w)
                pso = ((h // 2) % 8) * 64 + (512 if h >= 16 else 0)
                w = [(sV, ktm_v)]
                if h in (0, 16) and s_ev:
                    w.append((sV, s_ev))
                p.plan("tensor", lambda t, h=h, bi=bi, po=po, pso=pso: t.matmul(
                    ps_s[po:po + 64, pso:pso + 64],
                    kt_tm[:, h * 64:(h + 1) * 64],
                    v_ch[bi][:, h * 64:(h + 1) * 64],
                    start=(h in (0, 1, 16, 17)), stop=True),
                    waits=w, inc=((sT, 1) if h in (15, 31) else None))
                if h == 15:
                    sdel_grp[0] = p.val(sT)
                if h == 31:
                    sdel_grp[1] = p.val(sT)
                if h % 8 == 7:
                    vv = p.val(sV)
                    w = [(sT, ygrp_tok[grp]), (sV, vv)]
                    if grp == 0 and ydma[bi]:
                        w.append((sD, ydma[bi]))
                    p.plan("vector", lambda v, grp=grp, yb=yb, bi=bi:
                           v.tensor_add(
                               ytile[bi][:, grp * 512:(grp + 1) * 512],
                               ps_y[yb][:, :], dvb[yb][:, :]),
                           waits=w, inc=(sV, 1))
                    y_ev[yb] = (sV, p.val(sV))
            if "sp_praw" in self._taps:
                praw = getattr(self, "_praw_buf", None)
                if praw is None:
                    praw = self.sb("praw", (128, 1024), F32, es)
                    self._praw_buf = praw
                p.plan("vector", lambda v, praw=praw: v.tensor_copy(
                    praw[:, :], ps_s[:, :]),
                    waits=[(sT, sdel_grp[1])], inc=(sV, 1))
                p.dma("sync", self.sp_praw[ch, :, :], praw[:, :], sD,
                      waits=[(sV, p.val(sV))])
            p.plan("vector", lambda v: v.tensor_mul(
                stmp[:, 0:512], ps_s[:, 0:512], sdel_x[:, 0:512]),
                waits=[(sT, sdel_grp[0]), (sV, exp_v)])
            p.plan("vector", lambda v: v.tensor_mul(
                stmp[:, 512:1024], ps_s[:, 512:1024], sdel_x[:, 512:1024]),
                waits=[(sT, sdel_grp[1])])
            p.plan("vector", lambda v, S_cur=S_cur: v.tensor_mul(
                s_t2[:, :], S_cur[:, :], al_x[:, :]))
            p.vdrain()
            p.plan("vector", lambda v, S_nxt=S_nxt: v.tensor_add(
                S_nxt[:, :], s_t2[:, :], stmp[:, :]), inc=(sV, 1))
            s_ev = p.val(sV)
            if "sp_alx" in self._taps:
                p.dma("sync", self.sp_alx[ch, :, :], al_x[:, :], sD,
                      waits=[(sV, s_ev)])
                p.dma("sync", self.sp_st2[ch, :, :], s_t2[:, :], sD)
                p.dma("sync", self.sp_stm[ch, :, :], stmp[:, :], sD)
                tapd = p.val(sD)
                p.plan("vector", None, waits=[(sD, tapd)])
            if "sp_sall" in self._taps:
                p.dma("sync", self.sp_sall[ch, :, :], S_nxt[:, :], sD,
                      waits=[(sV, s_ev)])
            yv = max(y_ev[0][1], y_ev[1][1])
            p.dma("sync", self.sp_y[ch, :, :], ytile[bi][:, :], sD,
                  waits=[(sV, yv)])
            p.dma("sync", self.sp_rp[ch, :, :], rpb[bi][:, :], sD)
            ydma[bi] = p.val(sD)
            endT[bi] = p.val(sT)
            prevT = p.val(sT)
            if ch + 2 < NCH:
                chunk_load(ch + 2)

        S_fin = self.S_a
        p.plan("vector", lambda v: v.tensor_scalar(
            csc[:, 0:1024], S_fin[:, :], self.cmask_sb[:, 0:1], None,
            ALU.mult), waits=[(sV, s_ev)], inc=(sV, 1))
        vv = p.val(sV)
        p.dma("sync", self.cc_in[:, :], csc[:, 0:1024], sD, waits=[(sV, vv)])
        p.dma("sync", self.sp_sown[:, :], S_fin[:, :], sD)
        dd = p.val(sD)
        if USE_CC:
            p.plan("gpsimd", lambda g: g.collective_compute(
                "AllReduce", ALU.add,
                replica_groups=[[0, 1], [2, 3], [4, 5], [6, 7]],
                ins=[self.cc_in], outs=[self.cc_out]),
                waits=[(sD, dd)], inc=(sC, 1))
        else:
            p.plan("gpsimd", None, waits=[(sD, dd)], inc=(sC, 16))
        es.close()

    def p5_ffn1(self):
        nc, p = self.nc, self.p
        sD, sG, sT, sV, sS = self.sD, self.sG, self.sT, self.sV, self.sS
        es = contextlib.ExitStack()
        xnb_sb = self.sb("xnb_sb", (128, NCC * TOWN), BF16, es)
        wgb = [self.sb(f"wgb{i}", (128, 16 * 512), BF16, es) for i in range(2)]
        wub = [self.sb(f"wub{i}", (128, 16 * 512), BF16, es) for i in range(2)]
        sg_sb = self.sb("sg_sb", (128, 64), F32, es)
        su_sb = self.sb("su_sb", (128, 64), F32, es)
        gbuf = [self.sb(f"gb{i}", (128, 512), BF16, es) for i in range(2)]
        ubuf = [self.sb(f"ub{i}", (128, 512), BF16, es) for i in range(2)]
        fb = [self.sb(f"fb{i}", (128, TOWN), BF16, es) for i in range(2)]
        ps_g = [self.ps(f"p5g{i}", (128, 512), F32, es) for i in range(2)]
        ps_u = [self.ps(f"p5u{i}", (128, 512), F32, es) for i in range(2)]
        xnb3 = r3(xnb_sb, TOWN)

        p.dma("sync", xnb3[:, :, :],
              self.sp_xnb.rearrange("a p t -> p a t"), sD)
        p.dma("sync", sg_sb[:, :], self.sg_in, sD)
        ld = p.dma("sync", su_sb[:, :], self.su_in, sD)
        wtok = [0, 0]
        fdma = [0, 0]
        ev_g = [None, None]
        ev_u = [None, None]
        fuse_tok = [0, 0]
        fuse_v = 0
        for os_ in range(16):
            bi = os_ % 2
            w = [(sT, wtok[bi])] if wtok[bi] else []
            p.dma("gpsimd", r3(wgb[bi], 512)[:, :, :],
                  self.wgate[os_].rearrange("(a p) o -> p a o", p=128), sG,
                  waits=w)
            gd = p.dma("gpsimd", r3(wub[bi], 512)[:, :, :],
                       self.wup[os_].rearrange("(a p) o -> p a o", p=128), sG)
            for o128 in range(4):
                oc = os_ * 4 + o128
                fbi = oc % 2
                for tN in range(4):
                    bk = (o128 * 4 + tN) % 2
                    w = [(sG, gd), (sD, ld)]
                    if ev_g[bk] is not None:
                        w.append(ev_g[bk])
                    for cc in range(NCC):
                        p.plan("tensor", lambda t, cc=cc, o128=o128, tN=tN,
                               bk=bk, bi=bi: t.matmul(
                                   ps_g[bk][:, :],
                                   r3(wgb[bi], 512)
                                   [:, cc, o128 * 128:(o128 + 1) * 128],
                                   xnb3[:, cc, tN * 512:(tN + 1) * 512],
                                   start=(cc == 0), stop=(cc == NCC - 1)),
                               waits=(w if cc == 0 else ()))
                    w = []
                    if ev_u[bk] is not None:
                        w.append(ev_u[bk])
                    for cc in range(NCC):
                        p.plan("tensor", lambda t, cc=cc, o128=o128, tN=tN,
                               bk=bk, bi=bi: t.matmul(
                                   ps_u[bk][:, :],
                                   r3(wub[bi], 512)
                                   [:, cc, o128 * 128:(o128 + 1) * 128],
                                   xnb3[:, cc, tN * 512:(tN + 1) * 512],
                                   start=(cc == 0), stop=(cc == NCC - 1)),
                               waits=(w if cc == 0 else ()),
                               inc=((sT, 1) if cc == NCC - 1 else None))
                    tt = p.val(sT)
                    wf = [(sT, tt)]
                    if fuse_tok[bk]:
                        wf.append((sV, fuse_tok[bk]))
                    p.plan("scalar", lambda s, oc=oc, bk=bk: s.activation(
                        gbuf[bk][:, :], ps_g[bk][:, :], AF.Silu,
                        scale=sg_sb[:, oc:oc + 1]), waits=wf, inc=(sS, 1))
                    ev_g[bk] = (sS, p.val(sS))
                    p.plan("scalar", lambda s, oc=oc, bk=bk: s.activation(
                        ubuf[bk][:, :], ps_u[bk][:, :], AF.Copy,
                        scale=su_sb[:, oc:oc + 1]), inc=(sS, 1))
                    ev_u[bk] = (sS, p.val(sS))
                    sv = p.val(sS)
                    wv = [(sS, sv)]
                    if tN == 0 and fdma[fbi]:
                        wv.append((sD, fdma[fbi]))
                    p.plan("vector", lambda v, tN=tN, fbi=fbi, bk=bk:
                           v.tensor_mul(fb[fbi][:, tN * 512:(tN + 1) * 512],
                                        gbuf[bk][:, :], ubuf[bk][:, :]),
                           waits=wv, inc=(sV, 1))
                    fuse_tok[bk] = p.val(sV)
                    fuse_v = p.val(sV)
                p.dma("sync", self.sp_fuse[oc, :, :], fb[fbi][:, :], sD,
                      waits=[(sV, fuse_v)])
                fdma[fbi] = p.val(sD)
            wtok[bi] = p.val(sT)
        es.close()

    def p6_gn(self):
        nc, p = self.nc, self.p
        sD, sG, sT, sV, sS = self.sD, self.sG, self.sT, self.sV, self.sS
        sC = self.sC
        es = contextlib.ExitStack()
        wo_sb = self.sb("wo_sb", (128, NCC * C), BF16, es)
        lnw_sb = self.sb("lnw_sb", (128, C), F32, es)
        lnb_sb = self.sb("lnb_sb", (128, C), F32, es)
        sir = self.sb("sir", (128, 1024), F32, es)
        ysb = [self.sb(f"ysb{i}", (128, C), F32, es) for i in range(2)]
        rp_sb = [self.sb(f"rps{i}", (128, C), BF16, es) for i in range(2)]
        g_sb = [self.sb(f"gsb{i}", (128, C), BF16, es) for i in range(2)]
        x_sb = [self.sb(f"xsb{i}", (128, C), F32, es) for i in range(2)]
        yf = self.sb("yf", (128, C), F32, es)
        yn = self.sb("yn", (128, C), F32, es)
        yg = self.sb("yg", (128, C), BF16, es)
        ygT = self.sb("ygT", (128, C), BF16, es)
        gst = self.sb("gst", (128, 8 * 32), F32, es)
        partb = [self.sb(f"pb{i}", (128, C), F32, es) for i in range(2)]
        ps_c = [self.ps(f"p6c{i}", (128, 512), F32, es) for i in range(4)]
        ps_t = self.ps("p6t", (128, 1024), BF16, es)
        self.Sinit_b = self.sb("Sinit_b", (128, 1024), BF16, es)
        si_ev = self.sb("si_ev", (128, 1024), BF16, es)
        si_od = self.sb("si_od", (128, 1024), BF16, es)
        wo3 = r3(wo_sb, C)

        gd = p.dma("gpsimd", wo3[:, :, :],
                   self.w_o.rearrange("(a p) j -> p a j", p=128), sG)
        p.dma("sync", lnw_sb[:, :], self.lnw_in, sD)
        dln = p.dma("sync", lnb_sb[:, :], self.lnb_in, sD)
        cw = p.val(sC)
        if SINIT_ZERO:
            p.plan("vector", lambda v: v.memset(self.Sinit_b[:, :], 0.0))
            p.plan("vector", lambda v: v.memset(si_ev[:, :], 0.0))
            p.plan("vector", lambda v: v.memset(si_od[:, :], 0.0),
                   inc=(sV, 1))
            p.plan("scalar", None, waits=[(sV, p.val(sV))], inc=(sS, 1))
        else:
            gd2 = p.dma("gpsimd", sir[:, :], self.cc_out, sG, waits=[(sC, cw)])
            p.plan("vector", lambda v: v.tensor_scalar(
                sir[:, :], sir[:, :], self.imask_sb[:, 0:1], None, ALU.mult),
                waits=[(sG, gd2)], inc=(sV, 1))
            vv = p.val(sV)
            p.plan("scalar", lambda s: s.activation(
                self.Sinit_b[:, :], sir[:, :], AF.Copy), waits=[(sV, vv)],
                inc=(sS, 1))
        sv0 = p.val(sS)
        p.sdrain()
        p.plan("vector", lambda v: v.memset(si_ev[:, :], 0.0))
        p.plan("vector", lambda v: v.memset(si_od[:, :], 0.0), inc=(sV, 1))
        vv0 = p.val(sV)
        p.plan("scalar", lambda s: s.activation(
            si_ev[0:64, :], self.Sinit_b[0:64, :], AF.Copy),
            waits=[(sV, vv0)])
        p.plan("scalar", lambda s: s.activation(
            si_od[64:128, :], self.Sinit_b[64:128, :], AF.Copy),
            inc=(sS, 1))
        si_tok = p.val(sS)

        ld_tok = [0, 0]
        endT = [0, 0]
        pdma = [0, 0]
        apply_prev = 0
        wo_prevT = 0

        def tc_load(tc):
            bi = tc % 2
            w = [(sT, endT[bi]), (sV, p.val(sV))] if tc >= 2 else []
            p.dma("sync", ysb[bi][:, :], self.sp_y[tc, :, :], sD, waits=w)
            p.dma("sync", rp_sb[bi][:, :], self.sp_rp[tc, :, :], sD)
            p.dma("sync", g_sb[bi][:, :], self.sp_g[tc, :, :], sD)
            p.dma("sync", x_sb[bi][:, :],
                  self.x_ext[128 + tc * 128:128 + (tc + 1) * 128, :], sD)
            ld_tok[bi] = p.val(sD)

        tc_load(0)
        tc_load(1)
        corr_ev = [None] * 4
        tr_ev = [None, None]
        wops_tok = [None] * 4
        for tc in range(NTC):
            bi = tc % 2
            grp_tok = [0] * 4
            if 'corr' not in P6_PARTS:
                continue
            for h in range(H):
                po = 64 * (h % 2)
                cc = h // 2
                bk = h // 8
                w = []
                if h % 8 == 0:
                    w = [(sD, ld_tok[bi]), (sS, si_tok)]
                    if corr_ev[bk] is not None:
                        w.append(corr_ev[bk])
                    if wops_tok[bk] is not None:
                        w.append(wops_tok[bk])
                if 'mm' in P6_PARTS:
                    p.plan("tensor", lambda t, h=h, po=po, cc=cc, bk=bk, bi=bi:
                           t.matmul(ps_c[bk][:, (h % 8) * 64:(h % 8) * 64 + 64],
                                    rp_sb[bi][:, cc * 128:(cc + 1) * 128],
                                    (si_ev if h % 2 == 0 else si_od)
                                    [:, 64 * (h // 2):64 * (h // 2) + 64],
                                    start=(h % 8 == 0), stop=True),
                           waits=w, inc=((sT, 1) if h % 8 == 7 else None))
                if h % 8 == 7:
                    grp_tok[h // 8] = p.val(sT)
            if 'yf' not in P6_PARTS:
                continue
            for i in range(4):
                w = [(sT, grp_tok[i])]
                if apply_prev and i == 0:
                    w.append((sS, apply_prev))
                p.plan("vector", lambda v, i=i, bi=bi: v.tensor_add(
                    yf[:, i * 512:(i + 1) * 512],
                    ysb[bi][:, i * 512:(i + 1) * 512],
                    ps_c[i][:, :] if 'mm' in P6_PARTS else ysb[bi][:, i * 512:(i + 1) * 512]),
                    waits=w, inc=(sV, 1))
                corr_ev[i] = (sV, p.val(sV))
            if 'gn' not in P6_PARTS:
                continue
            p.vdrain()
            p.plan("vector", lambda v: v.tensor_mul(
                yn[:, :], yf[:, :], yf[:, :]))
            p.vdrain()
            for h in range(H):
                p.plan("vector", lambda v, h=h: v.reduce_sum(
                    gst[:, h:h + 1], yf[:, h * 64:(h + 1) * 64], axis=AX))
                p.plan("vector", lambda v, h=h: v.reduce_sum(
                    gst[:, 32 + h:33 + h], yn[:, h * 64:(h + 1) * 64],
                    axis=AX))
            p.vdrain()
            p.plan("vector", lambda v: v.tensor_scalar(
                gst[:, 64:96], gst[:, 0:32], 1.0 / HN, None, ALU.mult))
            p.plan("vector", lambda v: v.tensor_scalar(
                gst[:, 96:128], gst[:, 32:64], 1.0 / HN, None, ALU.mult))
            p.vdrain()
            p.plan("vector", lambda v: v.tensor_mul(
                gst[:, 128:160], gst[:, 64:96], gst[:, 64:96]))
            p.vdrain()
            p.plan("vector", lambda v: v.tensor_sub(
                gst[:, 160:192], gst[:, 96:128], gst[:, 128:160]))
            p.vdrain()
            p.plan("vector", lambda v: v.tensor_scalar(
                gst[:, 160:192], gst[:, 160:192], EPS_GN, None, ALU.add),
                inc=(sV, 1))
            vv = p.val(sV)
            p.plan("scalar", lambda s: s.activation(
                gst[:, 192:224], gst[:, 160:192], AF.Sqrt),
                waits=[(sV, vv)], inc=(sS, 1))
            sv = p.val(sS)
            p.plan("vector", lambda v: v.reciprocal(
                gst[:, 224:256], gst[:, 192:224]), waits=[(sS, sv)])
            p.vdrain()
            p.plan("vector", lambda v: v.tensor_mul(
                gst[:, 128:160], gst[:, 64:96], gst[:, 224:256]))
            p.vdrain()
            p.plan("vector", lambda v: v.tensor_scalar(
                gst[:, 128:160], gst[:, 128:160], -1.0, None, ALU.mult))
            p.vdrain()
            for h in range(H):
                p.plan("vector", lambda v, h=h: v.tensor_scalar(
                    yn[:, h * 64:(h + 1) * 64], yf[:, h * 64:(h + 1) * 64],
                    gst[:, 224 + h:225 + h], gst[:, 128 + h:129 + h],
                    ALU.mult, ALU.add),
                    inc=((sV, 1) if h == H - 1 else None))
            apply_prev = 0
            p.vdrain()
            p.plan("vector", lambda v: v.tensor_mul(
                yn[:, :], yn[:, :], lnw_sb[:, :]), waits=[(sD, dln)])
            p.vdrain()
            p.plan("vector", lambda v: v.tensor_add(
                yn[:, :], yn[:, :], lnb_sb[:, :]))
            p.vdrain()
            p.plan("vector", lambda v, bi=bi: v.tensor_mul(
                yg[:, :], yn[:, :], g_sb[bi][:, :]), inc=(sV, 1))
            yg_v = p.val(sV)
            if 'tr' not in P6_PARTS:
                continue
            for half in range(2):
                for j in range(8):
                    cc = half * 8 + j
                    w = [(sV, yg_v)]
                    if j == 0 and tr_ev[0] is not None:
                        w.append(tr_ev[0])
                    p.plan("tensor", lambda t, cc=cc, j=j: t.transpose(
                        ps_t[:, j * 128:(j + 1) * 128],
                        yg[:, cc * 128:(cc + 1) * 128], self.idbf[:, :]),
                        waits=w, inc=((sT, 1) if j == 7 else None))
                tt = p.val(sT)
                w = [(sT, tt)]
                if wo_prevT:
                    w.append((sT, wo_prevT))
                p.plan("vector", lambda v, half=half: v.tensor_copy(
                    ygT[:, half * 1024:(half + 1) * 1024], ps_t[:, :]),
                    waits=w, inc=(sV, 1))
                tr_ev[0] = (sV, p.val(sV))
            ygT_v = tr_ev[0][1]
            if 'wo' not in P6_PARTS:
                continue
            for cN in range(4):
                w = [(sV, ygT_v), (sG, gd)]
                if wops_tok[cN] is not None:
                    w.append(wops_tok[cN])
                w.append(corr_ev[cN])
                for jc in range(NCC):
                    p.plan("tensor", lambda t, jc=jc, cN=cN: t.matmul(
                        ps_c[cN][:, :], ygT[:, jc * 128:(jc + 1) * 128],
                        wo3[:, jc, cN * 512:(cN + 1) * 512],
                        start=(jc == 0), stop=(jc == NCC - 1)),
                        waits=(w if jc == 0 else ()),
                        inc=((sT, 1) if jc == NCC - 1 else None))
                tt = p.val(sT)
                wd = [(sT, tt)]
                if pdma[bi] and cN == 0:
                    wd.append((sD, pdma[bi]))
                p.plan("vector", lambda v, cN=cN, bi=bi: v.tensor_add(
                    partb[bi][:, cN * 512:(cN + 1) * 512], ps_c[cN][:, :],
                    x_sb[bi][:, cN * 512:(cN + 1) * 512]),
                    waits=wd, inc=(sV, 1))
                wops_tok[cN] = (sV, p.val(sV))
            wo_prevT = p.val(sT)
            vv = p.val(sV)
            p.dma("sync", self.sp_part[tc, :, :], partb[bi][:, :], sD,
                  waits=[(sV, vv)])
            pdma[bi] = p.val(sD)
            endT[bi] = p.val(sT)
            if tc + 2 < NTC:
                tc_load(tc + 2)
        es.close()

    def p7_ffn2(self):
        nc, p = self.nc, self.p
        sD, sG, sT, sV, sS = self.sD, self.sG, self.sT, self.sV, self.sS
        es = contextlib.ExitStack()
        fuse_sb = self.sb("fuse_sb", (128, NOC * 1024), BF16, es)
        sdn_sb = self.sb("sdn_sb", (128, C), F32, es)
        wdb = [self.sb(f"wdb{i}", (128, 2 * 512), BF16, es) for i in range(3)]
        pp = [self.sb(f"pp{i}", (128, 512), F32, es) for i in range(2)]
        ob = [self.sb(f"ob{i}", (128, 512), F32, es) for i in range(2)]
        psb = [self.ps(f"p7_{i}", (128, 512), F32, es) for i in range(8)]
        fuse3 = r3(fuse_sb, 1024)

        p.dma("sync", sdn_sb[:, :], self.sdown_in, sD)
        out_d = 0
        ob_d = [0, 0]
        ev_tok = [None] * 8
        mm_tok = [0, 0, 0]
        for tg in range(2):
            w = [(sT, p.val(sT))] if tg == 1 else []
            for i in range(8):
                p.dma("sync", fuse3[:, i * 8:(i + 1) * 8, :],
                      self.sp_fuse.rearrange("a p t -> p a t")
                      [:, i * 8:(i + 1) * 8, tg * 1024:(tg + 1) * 1024],
                      sD, waits=(w if i == 0 else ()))
            fld = p.val(sD)
            for cq in range(4):
                for Kp in range(32):
                    wi = Kp % 3
                    w = [(sT, mm_tok[wi])] if mm_tok[wi] else []
                    gdw = p.dma(
                        "gpsimd", r3(wdb[wi], 512)[:, :, :],
                        self.wdown.rearrange("(a p) c -> p a c", p=128)
                        [:, 2 * Kp:2 * Kp + 2, cq * 512:(cq + 1) * 512],
                        sG, waits=w)
                    for sub in range(2):
                        Ki = 2 * Kp + sub
                        for t8 in range(8):
                            w = []
                            if t8 == 0 and sub == 0:
                                w = [(sG, gdw), (sD, fld)]
                            if Ki == 0 and ev_tok[t8] is not None:
                                w.append(ev_tok[t8])
                            p.plan("tensor", lambda t, Ki=Ki, t8=t8, wi=wi,
                                   sub=sub: t.matmul(
                                       psb[t8][:, :],
                                       fuse3[:, Ki, t8 * 128:(t8 + 1) * 128],
                                       r3(wdb[wi], 512)[:, sub, :],
                                       start=(Ki == 0), stop=(Ki == NOC - 1)),
                                   waits=w,
                                   inc=((sT, 1) if (t8 == 7 and sub == 1)
                                        else None))
                    mm_tok[wi] = p.val(sT)
                kdone = p.val(sT)
                for t8 in range(8):
                    tc = tg * 8 + t8
                    bi = t8 % 2
                    w = [(sD, out_d)] if out_d else []
                    pl = p.dma("sync", pp[bi][:, :],
                               self.sp_part[tc, :, cq * 512:(cq + 1) * 512],
                               sD, waits=w)
                    w = [(sT, kdone), (sD, pl)]
                    if ob_d[bi]:
                        w.append((sD, ob_d[bi]))
                    p.plan("vector", lambda v, t8=t8, bi=bi, cq=cq:
                           v.tensor_mul(ob[bi][:, :], psb[t8][:, :],
                                        sdn_sb[:, cq * 512:(cq + 1) * 512]),
                           waits=w)
                    p.vdrain()
                    p.plan("vector", lambda v, bi=bi: v.tensor_add(
                        ob[bi][:, :], ob[bi][:, :], pp[bi][:, :]),
                        inc=(sV, 1))
                    vv = p.val(sV)
                    p.dma("sync",
                          self.out[tc * 128:(tc + 1) * 128,
                                   cq * 512:(cq + 1) * 512],
                          ob[bi][:, :], sD, waits=[(sV, vv)])
                    ev_tok[t8] = (sV, vv)
                    out_d = p.val(sD)
                    ob_d[bi] = out_d
        p.plan("sync", None, waits=[(sD, p.val(sD))])
        es.close()


_NC_CACHE = {}


def _get_nc(taps=(), stop_after=99):
    key = (tuple(sorted(taps)), stop_after)
    if key not in _NC_CACHE:
        _NC_CACHE[key] = Builder(key[0], stop_after).build()
    return _NC_CACHE[key]


def _prep_inputs(inputs):
    x = np.ascontiguousarray(np.asarray(inputs["L0_attn_x"], np.float32))
    f32 = np.float32

    def pcol(a):
        return np.ascontiguousarray(np.asarray(a, f32).reshape(16, 128).T)

    maa_np = np.stack([
        pcol(np.asarray(inputs["L0_attn_time_maa_x"]).reshape(-1)),
        pcol(np.asarray(inputs["L0_attn_time_maa_w"]).reshape(-1)),
        pcol(np.asarray(inputs["L0_attn_time_maa_k"]).reshape(-1)),
        pcol(np.asarray(inputs["L0_attn_time_maa_v"]).reshape(-1)),
        pcol(np.asarray(inputs["L0_attn_time_maa_r"]).reshape(-1)),
        pcol(np.asarray(inputs["L0_attn_time_maa_g"]).reshape(-1)),
        pcol(np.asarray(inputs["L0_attn_time_decay"]).reshape(-1)),
    ], axis=-1).reshape(128, 16 * 7).astype(f32)
    maa_np = np.ascontiguousarray(maa_np)

    w2 = np.asarray(inputs["L0_attn_w2"], f32)
    w2r = np.ascontiguousarray(w2.transpose(1, 0, 2).reshape(32, 5 * C))
    u_np = pcol(np.asarray(inputs["L0_attn_time_faaaa"]).reshape(-1))
    bcast = lambda a: np.ascontiguousarray(np.broadcast_to(
        np.asarray(a, f32).reshape(1, C), (128, C)))
    scol = lambda a: np.ascontiguousarray(
        np.asarray(a, f32).reshape(64, 128).T)
    common = {
        "w_r": np.ascontiguousarray(np.asarray(inputs["L0_attn_wr"], f32)),
        "w_k": np.ascontiguousarray(np.asarray(inputs["L0_attn_wk"], f32)),
        "w_v": np.ascontiguousarray(np.asarray(inputs["L0_attn_wv"], f32)),
        "w_g": np.ascontiguousarray(np.asarray(inputs["L0_attn_wg"], f32)),
        "w_o": np.ascontiguousarray(np.asarray(inputs["L0_attn_wo"], f32)),
        "w1": np.ascontiguousarray(np.asarray(inputs["L0_attn_w1"], f32)),
        "w2r": w2r,
        "td1": np.ascontiguousarray(np.asarray(inputs["L0_attn_td_w1"], f32)),
        "td2": np.ascontiguousarray(np.asarray(inputs["L0_attn_td_w2"], f32)),
        "maa": maa_np,
        "u_in": u_np,
        "lnw_bc": bcast(inputs["L0_attn_ln_w"]),
        "lnb_bc": bcast(inputs["L0_attn_ln_b"]),
        "sdown_bc": bcast(inputs["L0_ffn_s_down"]),
        "sg_in": scol(inputs["L0_ffn_s_gate"]),
        "su_in": scol(inputs["L0_ffn_s_up"]),
        "wgate": np.ascontiguousarray(
            np.asarray(inputs["L0_ffn_w_gate"]).reshape(C, 16, 512)
            .transpose(1, 0, 2)),
        "wup": np.ascontiguousarray(
            np.asarray(inputs["L0_ffn_w_up"]).reshape(C, 16, 512)
            .transpose(1, 0, 2)),
        "wdown": np.ascontiguousarray(np.asarray(inputs["L0_ffn_w_down"])),
    }
    in_maps = []
    for core in range(8):
        seq, half = core // 2, core % 2
        xo = np.asarray(x[seq, half * TOWN:(half + 1) * TOWN, :], f32)
        xext = np.zeros((TPAD, C), f32)
        xext[128:] = xo
        if half == 1:
            xext[127] = np.asarray(x[seq, TOWN - 1, :], f32)
        d = dict(common)
        d["x_ext"] = xext
        d["cmask"] = np.full((128, 1), 1.0 if half == 0 else 0.0, f32)
        d["imask"] = np.full((128, 1), 1.0 if half == 1 else 0.0, f32)
        in_maps.append(d)
    return in_maps


def kernel(**inputs):
    nc = _get_nc()
    in_maps = _prep_inputs(inputs)
    res = run_bass_kernel_spmd(nc, in_maps, core_ids=list(range(8)))
    full = np.empty((4, 4096, C), np.float32)
    for core in range(8):
        seq, half = core // 2, core % 2
        full[seq, half * TOWN:(half + 1) * TOWN, :] = res.results[core]["out"]
    return full
